# version 17
# baseline (speedup 1.0000x reference)
import numpy as np
import ml_dtypes
import concourse.bass as bass
import concourse.mybir as mybir
from concourse.bass_utils import run_bass_kernel_spmd
from contextlib import ExitStack

F32 = mybir.dt.float32
BF16 = mybir.dt.bfloat16
AF = mybir.ActivationFunctionType
ALU = mybir.AluOpType
AX = mybir.AxisListType

NCORE = 8
NQ = 4
TQ = 1024
NCH = 4
CH = 256
NH = 16
HD = 64
H = 1024
NHC = 8
NE = 8
FF = 4096
NFC = 32
NTT = 8
FB = 256
NFB = FF // FB
NWB = 2
EPS = 1e-5
MASKV = -1e30
QSCL = 0.125
PHASEA = True
PHASEB = True
PHA_DEPTH = 4

ENGS = ("sync", "gpsimd", "tensor", "vector", "scalar")


class Bld:
    def __init__(self):
        self.ops = {e: [] for e in ENGS}
        self.c = {}
        self.ev = {}
        self.psfree = {}

    def emit(self, eng, fn, waits=(), inc=None, key=None):
        if inc:
            sem, n = inc
            self.c[sem] = self.c.get(sem, 0) + n
            if key is not None:
                self.ev[key] = (sem, self.c[sem])
        self.ops[eng].append((list(waits), fn, inc))
        return self.c.get(inc[0]) if inc else None

    def w(self, key):
        return ("ev", key)

    def bankw(self, bank):
        return list(self.psfree.get(bank, []))

    def setbank(self, bank, key):
        self.psfree[bank] = [key]

    def addbank(self, bank, key):
        self.psfree.setdefault(bank, []).append(key)


def build_program():
    nc = bass.Bass("TRN2", target_bir_lowering=False)
    d = {}
    d["x"] = nc.dram_tensor("x", [NQ * TQ, H], F32, kind="ExternalInput").ap()
    for n, shp in (("wq", [H, H]), ("wk", [H, H]), ("wv", [H, H]), ("wo", [H, H]),
                   ("wr", [H, 8]), ("wgt", [H, 8]),
                   ("wgate", [NE, H, FF]), ("wup", [NE, H, FF]), ("wdown", [NE, FF, H]),
                   ("identf", [128, 128]), ("mask", [128, 512])):
        d[n] = nc.dram_tensor(n, shp, F32, kind="ExternalInput").ap()
    d["identb"] = nc.dram_tensor("identb", [128, 128], BF16, kind="ExternalInput").ap()
    out_d = nc.dram_tensor("out", [NQ * TQ, H], F32, kind="ExternalOutput").ap()

    ctx = ExitStack()
    sb = lambda name, shape, dt: ctx.enter_context(nc.sbuf_tensor(name, shape, dt))
    bigA = sb("bigA", [128, 32 * 1024], BF16)
    y = sb("y", [128, NTT * H], F32)
    accum = sb("accum", [128, NTT * H], F32)
    x2T = sb("x2T", [128, NHC * TQ], BF16)
    wgb = [sb(f"wgb{i}", [128, NHC * FB], BF16) for i in range(NWB)]
    wub = [sb(f"wub{i}", [128, NHC * FB], BF16) for i in range(NWB)]
    xsc = sb("xsc", [128, 2048], F32)
    arena = sb("arena", [128, 8704], F32)
    svec = sb("svec", [128, 48], F32)
    ssc = sb("ssc", [128, 48], F32)
    sv8 = sb("sv8", [128, 16], F32)
    rg_sb = sb("rg", [128, 16], F32)
    cw_sb = sb("cw", [128, NTT * 8], F32)
    wrgb = sb("wrgb", [128, NHC * 16], BF16)
    identf_sb = sb("identfs", [128, 128], F32)
    identb_sb = sb("identbs", [128, 128], BF16)
    mask_sb = sb("masks", [128, 512], F32)
    eps_sb = sb("epss", [128, 1], F32)

    arB = arena[:].bitcast(BF16)
    x1buf = arena[:, 0:1024]
    x2buf = arena[:, 0:1024]
    x2buf2 = arena[:, 7680:8704]
    dumpA = arena[:, 1024:2048]
    s_sb = arena[:, 2048:3072]
    dump3 = arena[:, 3072:4096]
    wdst = [arena[:, 1024:2048], arena[:, 2048:3072]]
    sg = [arena[:, 3072 + i * 512: 3072 + (i + 1) * 512] for i in range(4)]
    qT = arB[:, 6144:8192]
    kT = arB[:, 8192:10240]
    v_sb = arB[:, 10240:12288]
    wdbf = arB[:, 10240:12288]
    attnT = arB[:, 12288:14336]
    x1T = arB[:, 14336:16384]
    pT = [arB[:, 16384:16896], arB[:, 16896:17408]]
    pn = arB[:, 2048:4096]

    P = [ctx.enter_context(nc.psum_tensor(f"P{i}", [128, 512], F32)) for i in range(8)]
    Pb = [P[6][:].bitcast(BF16), P[7][:].bitcast(BF16)]

    sems = {}
    for s in ("cst", "wrg", "awq", "awk", "awv", "awo", "xs0", "xs1", "xsc",
              "wgu0", "wgu1", "wgu2", "wd0", "wd1", "wd2", "st", "pe", "vec", "sc"):
        sems[s] = ctx.enter_context(nc.semaphore(s))

    b = Bld()
    E = b.emit
    W = b.w

    WQ0, WK0, WV0, WO0 = 0, 8192, 16384, 24576

    E("vector", lambda e: e.memset(eps_sb[:], EPS), inc=("vec", 1), key="eps")
    E("sync", lambda e: e.dma_start(out=identf_sb[:], in_=d["identf"]), inc=("cst", 16), key="identf")
    E("sync", lambda e: e.dma_start(out=identb_sb[:], in_=d["identb"]), inc=("cst", 16), key="identb")
    E("sync", lambda e: e.dma_start(out=mask_sb[:], in_=d["mask"]), inc=("cst", 16), key="mask")
    wrg3 = wrgb[:].rearrange("p (hc e) -> p hc e", hc=NHC)
    E("gpsimd", lambda e: e.dma_start(
        out=wrg3[:, :, 0:8], in_=d["wr"].rearrange("(hc p) e -> p hc e", p=128)), inc=("wrg", 16))
    E("gpsimd", lambda e: e.dma_start(
        out=wrg3[:, :, 8:16], in_=d["wgt"].rearrange("(hc p) e -> p hc e", p=128)), inc=("wrg", 16), key="wrg")
    b.ev["identf"] = ("cst", 48)
    b.ev["identb"] = ("cst", 48)
    b.ev["mask"] = ("cst", 48)

    x5 = d["x"].rearrange("(q c tt p) h -> q c p tt h", q=NQ, c=NCH, tt=2, p=128)
    out4 = out_d.rearrange("(q tt p) h -> q p tt h", q=NQ, tt=NTT, p=128)
    yv = y[:].rearrange("p (tt h) -> p tt h", tt=NTT)
    x1Tv = x1T.rearrange("p (hc tt f) -> p hc tt f", hc=NHC, tt=2)
    x2Tv = x2T[:].rearrange("p (hc t) -> p hc t", hc=NHC)

    def lnchain(tag, q, s, srcs, src_waits_v, src_waits_s, dump, n=1024.0, extra_sq_waits=(), psum_src=False):
        sv, sss = s * 8, s * 8
        kk = None
        for i, (ap, w_) in enumerate(srcs):
            kk = E("vector", lambda e, ap=ap, i=i: e.reduce_sum(svec[:, sv + i:sv + i + 1], ap, axis=AX.X),
                   waits=src_waits_v if i == 0 else (), inc=("vec", 1), key=f"{tag}sum{i}")
        if len(srcs) == 2:
            E("vector", lambda e: e.tensor_add(svec[:, sv:sv + 1], svec[:, sv:sv + 1], svec[:, sv + 1:sv + 2]),
              waits=[W(f"{tag}sum1")], inc=("vec", 1), key=f"{tag}sumt")
        for i, (ap, w_) in enumerate(srcs):
            sqw = (list(src_waits_s) + list(extra_sq_waits)) if i == 0 else []
            if psum_src:
                sqw = sqw + [W(f"{tag}sum{i}")]
            E("scalar", lambda e, ap=ap, i=i, w_=w_: e.activation(
                dump[:, i * w_:(i + 1) * w_], ap, AF.Square, accum_out=ssc[:, sss + i:sss + i + 1]),
                waits=sqw,
                inc=("sc", 1) if i == len(srcs) - 1 else None, key=f"{tag}sqa" if i == len(srcs) - 1 else None)
        E("scalar", lambda e: e.activation(ssc[:, sss + 2:sss + 2 + len(srcs)], ssc[:, sss:sss + len(srcs)],
                                           AF.Identity), waits=[W(f"{tag}sqa")], inc=("sc", 1), key=f"{tag}sq")
        if len(srcs) == 2:
            E("vector", lambda e: e.tensor_add(svec[:, sv + 6:sv + 7], ssc[:, sss + 2:sss + 3],
                                               ssc[:, sss + 3:sss + 4]),
              waits=[W(f"{tag}sq")], inc=("vec", 1), key=f"{tag}sqt")
            sqsrc = svec[:, sv + 6:sv + 7]
        else:
            sqsrc = ssc[:, sss + 2:sss + 3]
        E("vector", lambda e: e.tensor_scalar(svec[:, sv + 2:sv + 3], svec[:, sv:sv + 1], 1.0 / n, None, ALU.mult),
          waits=[W(f"{tag}sumt" if len(srcs) == 2 else f"{tag}sum0")], inc=("vec", 1), key=f"{tag}m")
        E("vector", lambda e: e.tensor_mul(svec[:, sv + 3:sv + 4], svec[:, sv + 2:sv + 3], svec[:, sv + 2:sv + 3]),
          waits=[W(f"{tag}m")], inc=("vec", 1), key=f"{tag}m2")
        E("vector", lambda e, sqsrc=sqsrc: e.scalar_tensor_tensor(svec[:, sv + 4:sv + 5], sqsrc, 1.0 / n,
                                                     svec[:, sv + 3:sv + 4], ALU.mult, ALU.subtract),
          waits=[W(f"{tag}m2"), W(f"{tag}sqt" if len(srcs) == 2 else f"{tag}sq")],
          inc=("vec", 1), key=f"{tag}var")
        E("scalar", lambda e: e.activation(ssc[:, sss + 4:sss + 5], svec[:, sv + 4:sv + 5], AF.Ln,
                                           bias=eps_sb[:, 0:1]),
          waits=[W(f"{tag}var")], inc=("sc", 1), key=f"{tag}ln")
        E("scalar", lambda e: e.activation(ssc[:, sss + 5:sss + 6], ssc[:, sss + 4:sss + 5], AF.Exp, scale=-0.5),
          waits=[W(f"{tag}ln")], inc=("sc", 1), key=f"{tag}r")
        E("vector", lambda e: e.scalar_tensor_tensor(svec[:, sv + 5:sv + 6], svec[:, sv + 2:sv + 3], -1.0,
                                                     ssc[:, sss + 5:sss + 6], ALU.mult, ALU.mult),
          waits=[W(f"{tag}r")], inc=("vec", 1), key=f"{tag}nmr")
        return ssc[:, sss + 5:sss + 6], svec[:, sv + 5:sv + 6], f"{tag}nmr", f"{tag}r", svec[:, sv + 2:sv + 3]

    def lnchain2(tagA, tagB, srcA, srcB, wA, wsA, wB, wsB, dump2, lwA=(), lwB=(), base=32):
        out = {}
        for s2, (tag, src, wv, wsc) in enumerate(((tagA, srcA, wA, wsA), (tagB, srcB, wB, wsB))):
            E("vector", lambda e, src=src, s2=s2: e.reduce_sum(svec[:, base + s2 * 8:base + s2 * 8 + 1], src, axis=AX.X),
              waits=wv, inc=("vec", 1), key=f"{tag}sum0")
        for s2, (tag, src, wv, wsc) in enumerate(((tagA, srcA, wA, wsA), (tagB, srcB, wB, wsB))):
            ex = list(lwA) if s2 == 0 else list(lwB)
            E("scalar", lambda e, src=src, s2=s2: e.activation(
                dump2[s2], src, AF.Square,
                accum_out=ssc[:, base + s2 * 8:base + s2 * 8 + 1]),
                waits=list(wsc) + ex,
                inc=("sc", 1), key=f"{tag}sqa")
        for s2, tag in enumerate((tagA, tagB)):
            E("scalar", lambda e, s2=s2: e.activation(ssc[:, base + s2 * 8 + 2:base + s2 * 8 + 3], ssc[:, base + s2 * 8:base + s2 * 8 + 1],
                                                      AF.Identity), waits=[W(f"{tag}sqa")], inc=("sc", 1), key=f"{tag}sq")
        for s2, tag in enumerate((tagA, tagB)):
            E("vector", lambda e, s2=s2: e.tensor_scalar(svec[:, base + s2 * 8 + 2:base + s2 * 8 + 3], svec[:, base + s2 * 8:base + s2 * 8 + 1],
                                                         1.0 / 1024, None, ALU.mult),
              waits=[W(f"{tag}sum0")], inc=("vec", 1), key=f"{tag}m")
        for s2, tag in enumerate((tagA, tagB)):
            E("vector", lambda e, s2=s2: e.tensor_mul(svec[:, base + s2 * 8 + 3:base + s2 * 8 + 4], svec[:, base + s2 * 8 + 2:base + s2 * 8 + 3],
                                                      svec[:, base + s2 * 8 + 2:base + s2 * 8 + 3]),
              waits=[W(f"{tag}m")], inc=("vec", 1), key=f"{tag}m2")
        for s2, tag in enumerate((tagA, tagB)):
            E("vector", lambda e, s2=s2: e.scalar_tensor_tensor(svec[:, base + s2 * 8 + 4:base + s2 * 8 + 5],
                                                                ssc[:, base + s2 * 8 + 2:base + s2 * 8 + 3], 1.0 / 1024,
                                                                svec[:, base + s2 * 8 + 3:base + s2 * 8 + 4], ALU.mult, ALU.subtract),
              waits=[W(f"{tag}m2"), W(f"{tag}sq")], inc=("vec", 1), key=f"{tag}var")
        for s2, tag in enumerate((tagA, tagB)):
            E("scalar", lambda e, s2=s2: e.activation(ssc[:, base + s2 * 8 + 4:base + s2 * 8 + 5], svec[:, base + s2 * 8 + 4:base + s2 * 8 + 5],
                                                      AF.Ln, bias=eps_sb[:, 0:1]),
              waits=[W(f"{tag}var")], inc=("sc", 1), key=f"{tag}ln")
        for s2, tag in enumerate((tagA, tagB)):
            E("scalar", lambda e, s2=s2: e.activation(ssc[:, base + s2 * 8 + 5:base + s2 * 8 + 6], ssc[:, base + s2 * 8 + 4:base + s2 * 8 + 5],
                                                      AF.Exp, scale=-0.5),
              waits=[W(f"{tag}ln")], inc=("sc", 1), key=f"{tag}r")
        for s2, tag in enumerate((tagA, tagB)):
            E("vector", lambda e, s2=s2: e.scalar_tensor_tensor(svec[:, base + s2 * 8 + 5:base + s2 * 8 + 6],
                                                                svec[:, base + s2 * 8 + 2:base + s2 * 8 + 3], -1.0,
                                                                ssc[:, base + s2 * 8 + 5:base + s2 * 8 + 6], ALU.mult, ALU.mult),
              waits=[W(f"{tag}r")], inc=("vec", 1), key=f"{tag}nmr")
            out[tag] = (ssc[:, base + s2 * 8 + 5:base + s2 * 8 + 6], svec[:, base + s2 * 8 + 5:base + s2 * 8 + 6], f"{tag}nmr")
        return out

    def ln_pipe8(mktag, srcs8, wv8, ws8, dump_ap, pcol=0):
        res = {}
        for tt in range(8):
            E("vector", lambda e, tt=tt: e.reduce_sum(svec[:, 8 + tt:9 + tt], srcs8[tt], axis=AX.X),
              waits=wv8[tt], inc=("vec", 1), key=f"{mktag}{tt}sum")
        for tt in range(8):
            waw = [W(f"{mktag}{tt-1}sqa")] if tt > 0 else []
            E("scalar", lambda e, tt=tt: e.activation(dump_ap, srcs8[tt], AF.Square,
                                                      accum_out=ssc[:, 8 + tt:9 + tt]),
              waits=list(ws8[tt]) + waw, inc=("sc", 1), key=f"{mktag}{tt}sqa")
        for tt in range(8):
            E("scalar", lambda e, tt=tt: e.activation(ssc[:, 16 + tt:17 + tt], ssc[:, 8 + tt:9 + tt], AF.Identity),
              waits=[W(f"{mktag}{tt}sqa")], inc=("sc", 1), key=f"{mktag}{tt}sq")
        for tt in range(8):
            E("vector", lambda e, tt=tt: e.tensor_scalar(svec[:, 16 + tt:17 + tt], svec[:, 8 + tt:9 + tt],
                                                         1.0 / 1024, None, ALU.mult),
              waits=[W(f"{mktag}{tt}sum")], inc=("vec", 1), key=f"{mktag}{tt}m")
        for tt in range(8):
            E("vector", lambda e, tt=tt: e.tensor_mul(svec[:, 24 + tt:25 + tt], svec[:, 16 + tt:17 + tt],
                                                      svec[:, 16 + tt:17 + tt]),
              waits=[W(f"{mktag}{tt}m")], inc=("vec", 1), key=f"{mktag}{tt}m2")
        for tt in range(8):
            E("vector", lambda e, tt=tt: e.scalar_tensor_tensor(svec[:, 24 + tt:25 + tt], ssc[:, 16 + tt:17 + tt],
                                                                1.0 / 1024, svec[:, 24 + tt:25 + tt],
                                                                ALU.mult, ALU.subtract),
              waits=[W(f"{mktag}{tt}m2"), W(f"{mktag}{tt}sq")], inc=("vec", 1), key=f"{mktag}{tt}var")
        for tt in range(8):
            E("scalar", lambda e, tt=tt: e.activation(ssc[:, 24 + tt:25 + tt], svec[:, 24 + tt:25 + tt], AF.Ln,
                                                      bias=eps_sb[:, 0:1]),
              waits=[W(f"{mktag}{tt}var")], inc=("sc", 1), key=f"{mktag}{tt}ln")
        for tt in range(8):
            E("scalar", lambda e, tt=tt: e.activation(ssc[:, pcol + tt:pcol + tt + 1], ssc[:, 24 + tt:25 + tt],
                                                      AF.Exp, scale=-0.5),
              waits=[W(f"{mktag}{tt}ln")], inc=("sc", 1), key=f"{mktag}{tt}r")
        for tt in range(8):
            E("vector", lambda e, tt=tt: e.scalar_tensor_tensor(svec[:, pcol + tt:pcol + tt + 1],
                                                                svec[:, 16 + tt:17 + tt], -1.0,
                                                                ssc[:, pcol + tt:pcol + tt + 1],
                                                                ALU.mult, ALU.mult),
              waits=[W(f"{mktag}{tt}r")], inc=("vec", 1), key=f"{mktag}{tt}nmr")
            res[tt] = (ssc[:, pcol + tt:pcol + tt + 1], svec[:, pcol + tt:pcol + tt + 1], f"{mktag}{tt}nmr")
        return res

    gfb_count = 0
    for q in range(NQ):
        T = f"q{q}"
        if q == 0:
            aw_waits = []
        elif PHASEB:
            aw_waits = [W(f"s2end_q{q-1}")]
        else:
            aw_waits = [W(f"q{q-1}c{NCH-1}wo11")]
        for nm, base, src, semn in (("wq", WQ0, d["wq"], "awq"), ("wk", WK0, d["wk"], "awk"),
                                    ("wv", WV0, d["wv"], "awv"), ("wo", WO0, d["wo"], "awo")):
            dst = bigA[:, base:base + 8192].rearrange("p (hc h2) -> p hc h2", hc=NHC)
            E("gpsimd", lambda e, dst=dst, src=src: e.dma_start(
                out=dst, in_=src.rearrange("(hc p) h2 -> p hc h2", p=128)),
                waits=aw_waits, inc=(semn, 16), key=f"{T}{nm}")
            aw_waits = []
        for ee in range(NE if PHASEB else 0):
            for fb in range(NFB):
                gfb = gfb_count
                gfb_count += 1
                ws = []
                if gfb >= NWB:
                    ws = [W(f"s1fb{gfb - NWB}")]
                bi = gfb % NWB
                for wsrc, wdstb in ((d["wgate"], wgb), (d["wup"], wub)):
                    src = wsrc[ee].rearrange("(hc p) f -> p hc f", p=128)[:, :, fb * FB:(fb + 1) * FB]
                    dstv = wdstb[bi][:].rearrange("p (hc f) -> p hc f", hc=NHC)
                    E("gpsimd", lambda e, dstv=dstv, src=src: e.dma_start(out=dstv, in_=src),
                      waits=ws, inc=(f"wgu{gfb % NWB}", 16), key=f"wgu{gfb}" if wsrc is d["wup"] else None)
                    ws = []

        if not PHASEA:
            for c in range(NCH):
                C = f"{T}c{c}"
                xw = [W(f"store_q{q-1}")] if (q > 0 and c == 0) else []
                E("sync", lambda e, q=q, c=c: e.dma_start(out=yv[:, 2 * c:2 * c + 2, :], in_=x5[q, c]),
                  waits=xw, inc=("xs0", 16), key=f"{C}x")
        for c in range(NCH if PHASEA else 0):
            C = f"{T}c{c}"
            xw = [W(f"store_q{q-1}_tt{2*c+1}")] if q > 0 else []
            gc = q * NCH + c
            if gc >= 2:
                pq, pc = divmod(gc - 2, NCH)
                xw += [W(f"q{pq}c{pc}x1t1"), W(f"q{pq}c{pc}l1t1sum0")]
            E("sync", lambda e, q=q, c=c: e.dma_start(out=yv[:, 2 * c:2 * c + 2, :], in_=x5[q, c]),
              waits=xw, inc=(f"xs{c % 2}", 16), key=f"{C}x")

            if c == 0:
                ln1pend = {}

                def emit_ln1(c2, q=q, T=T, ln1pend=None):
                    C2 = f"{T}c{c2}"
                    lwA, lwB = [], []
                    if c2 > 0:
                        if PHA_DEPTH >= 3:
                            lwA = [W(f"{T}c{c2-1}pt{NH-1}")]
                            lwB = [W(f"{T}c{c2-1}norm{NH-1}")]
                    elif q > 0:
                        if PHA_DEPTH >= 3:
                            lwA = [W(f"q{q-1}c{NCH-1}pt{NH-1}")]
                            lwB = [W(f"q{q-1}c{NCH-1}norm{NH-1}")]
                        if PHASEB:
                            kbl = q * NE * NFC * 2 // 2
                            lwA = lwA + [W(f"wddma{kbl-1}"), W(f"wddma{kbl-2}")]
                            lwB = lwB + [W(f"wddma{kbl-1}"), W(f"wddma{kbl-2}")]
                    ln1pend[c2] = lnchain2(
                        f"{C2}l1t0", f"{C2}l1t1",
                        y[:, (2 * c2) * H:(2 * c2 + 1) * H], y[:, (2 * c2 + 1) * H:(2 * c2 + 2) * H],
                        [W(f"{C2}x")], [W(f"{C2}x")], [], [],
                        (dumpA, s_sb), lwA=lwA, lwB=lwB)
                if q == 0 or not (PHASEA and PHASEB and PHA_DEPTH >= 4):
                    emit_ln1(0, ln1pend=ln1pend)
            early = (c == 0 and q > 0 and PHASEA and PHASEB and PHA_DEPTH >= 4)
            rres1 = None if early else ln1pend[c]
            for tt2 in range(2):
                tt = 2 * c + tt2
                ytile = y[:, tt * H:(tt + 1) * H]
                x1b = x1buf if tt2 == 0 else dumpA
                if not early:
                    lw_pe = []
                    if tt2 == 0:
                        if c > 0:
                            lw_pe = [W(f"{T}c{c-1}trg01")]
                        elif q > 0:
                            lw_pe = [W(f"q{q-1}b{NTT-1}x2tr1") if PHASEB else W(f"q{q-1}c{NCH-1}trg01")]
                    else:
                        if PHA_DEPTH >= 3:
                            if c > 0:
                                lw_pe = [W(f"{T}c{c-1}pt{NH-1}"), W(f"{T}c{c-1}norm{NH-1}")]
                            elif q > 0:
                                lw_pe = [W(f"q{q-1}c{NCH-1}pt{NH-1}"), W(f"q{q-1}c{NCH-1}norm{NH-1}")]
                        if c == 0 and q == 0:
                            lw_pe = []
                        if c > 0 or q > 0:
                            lw_pe = lw_pe + [W(f"{T}c{c-1}trg11") if c > 0 else
                                             (W(f"q{q-1}c{NCH-1}trg11") if PHASEA else W(f"q{q-1}c{NCH-1}x"))]
                    r1, nmr1, k_nmr1 = rres1[f"{C}l1t{tt2}"]
                    E("scalar", lambda e, ytile=ytile, r1=r1, nmr1=nmr1, x1b=x1b: e.activation(
                        x1b, ytile, AF.Identity, bias=nmr1, scale=r1),
                        waits=[W(k_nmr1)] + lw_pe, inc=("sc", 1), key=f"{C}x1t{tt2}")
                for g2 in range(2):
                    bank = g2
                    for k4 in range(4):
                        hc = g2 * 4 + k4
                        ws = []
                        if k4 == 0:
                            ws = [W(f"{C}x1t{tt2}")] + b.bankw(bank) + ([W("identf")] if (q == 0 and c == 0 and tt2 == 0) else [])
                        E("tensor", lambda e, bank=bank, k4=k4, hc=hc, x1b=x1b: e.transpose(
                            P[bank][:, k4 * 128:(k4 + 1) * 128], x1b[:, hc * 128:(hc + 1) * 128], identf_sb[:]),
                            waits=ws, inc=("pe", 1) if k4 == 3 else None, key=f"{C}trg{tt2}{g2}" if k4 == 3 else None)
                    if g2 == 1:
                        b.ev[f"{C}tr{tt2}"] = b.ev[f"{C}trg{tt2}{g2}"]
                    E("vector", lambda e, g2=g2, tt2=tt2, bank=bank: e.tensor_copy(
                        x1Tv[:, g2 * 4:(g2 + 1) * 4, tt2, :],
                        P[bank][:, 0:512].rearrange("p (k f) -> p k f", k=4)),
                        waits=[W(f"{C}trg{tt2}{g2}")], inc=("vec", 1), key=f"{C}x1T{tt2}{g2}")
                    b.setbank(bank, W(f"{C}x1T{tt2}{g2}"))

            if PHA_DEPTH < 2:
                if c + 1 < NCH:
                    emit_ln1(c + 1, ln1pend=ln1pend)
                continue
            gi = 0
            for proj in ("q", "k"):
                wbase = WQ0 if proj == "q" else WK0
                dstT = qT if proj == "q" else kT
                for dq in range(8):
                    bank = 4 + gi % 2
                    for hc in range(NHC):
                        ws = []
                        if hc == 0:
                            ws = [W(f"{C}x1T11")] + b.bankw(bank)
                            if c == 0 and gi == 0:
                                ws.append(W(f"{T}wq"))
                            if c == 0 and proj == "k" and dq == 0:
                                ws.append(W(f"{T}wk"))
                        E("tensor", lambda e, bank=bank, hc=hc, dq=dq, wbase=wbase: e.matmul(
                            P[bank][:, 0:256], bigA[:, wbase + hc * 1024 + dq * 128: wbase + hc * 1024 + (dq + 1) * 128],
                            x1T[:, hc * 256:(hc + 1) * 256], start=(hc == 0), stop=(hc == NHC - 1)),
                            waits=ws, inc=("pe", 1) if hc == NHC - 1 else None,
                            key=f"{C}prj{gi}" if hc == NHC - 1 else None)
                    if proj == "q":
                        E("scalar", lambda e, bank=bank, dq=dq: e.activation(
                            qT[:, dq * 256:(dq + 1) * 256], P[bank][:, 0:256], AF.Identity, scale=QSCL),
                            waits=[W(f"{C}prj{gi}")], inc=("sc", 1), key=f"{C}qev{dq}")
                        b.setbank(bank, W(f"{C}qev{dq}"))
                    else:
                        E("vector", lambda e, bank=bank, dq=dq: e.tensor_copy(
                            kT[:, dq * 256:(dq + 1) * 256], P[bank][:, 0:256]),
                            waits=[W(f"{C}prj{gi}")], inc=("vec", 1), key=f"{C}kev{dq}")
                        b.setbank(bank, W(f"{C}kev{dq}"))
                    gi += 1
            def emit_scores(h):
                par = h % 2
                dq = h // 2
                sbank = par
                bw = [W(f"{C}mask{h - 2}")] if h >= 2 else b.bankw(sbank)
                for it in range(2):
                    ws = []
                    if it == 0:
                        ws = [W(f"{C}qev{dq}"), W(f"{C}kev{dq}")] + bw
                    E("tensor", lambda e, sbank=sbank, it=it, par=par, dq=dq: e.matmul(
                        P[sbank][:, it * 256:(it + 1) * 256],
                        qT[64 * par:64 * par + 64, dq * 256 + it * 128: dq * 256 + it * 128 + 128],
                        kT[64 * par:64 * par + 64, dq * 256:(dq + 1) * 256],
                        start=True, stop=True),
                        waits=ws, inc=("pe", 1) if it == 1 else None, key=f"{C}sc{h}" if it == 1 else None)

            for tt2 in range(2):
                for dvh in range(2):
                    bank = 4 + gi % 2
                    for hc in range(NHC):
                        ws = []
                        if hc == 0:
                            ws = b.bankw(bank)
                            if c == 0 and tt2 == 0 and dvh == 0:
                                ws.append(W(f"{T}wv"))
                        E("tensor", lambda e, bank=bank, hc=hc, tt2=tt2, dvh=dvh: e.matmul(
                            P[bank][:, 0:512], x1T[:, hc * 256 + tt2 * 128: hc * 256 + tt2 * 128 + 128],
                            bigA[:, WV0 + hc * 1024 + dvh * 512: WV0 + hc * 1024 + (dvh + 1) * 512],
                            start=(hc == 0), stop=(hc == NHC - 1)),
                            waits=ws, inc=("pe", 1) if hc == NHC - 1 else None,
                            key=f"{C}prj{gi}" if hc == NHC - 1 else None)
                    E("vector", lambda e, bank=bank, tt2=tt2, dvh=dvh: e.tensor_copy(
                        v_sb[:, tt2 * 1024 + dvh * 512: tt2 * 1024 + (dvh + 1) * 512], P[bank][:, 0:512]),
                        waits=[W(f"{C}prj{gi}")], inc=("vec", 1), key=f"{C}vev{tt2}{dvh}")
                    b.setbank(bank, W(f"{C}vev{tt2}{dvh}"))
                    gi += 1

            if PHA_DEPTH < 3:
                if c + 1 < NCH:
                    emit_ln1(c + 1, ln1pend=ln1pend)
                continue
            emit_scores(0)
            emit_scores(1)

            def emit_front(h):
                par = h % 2
                sbank = par
                s_h = s_sb[:, par * 512:(par + 1) * 512]
                pn_h = pn[:, par * 512:(par + 1) * 512]
                E("vector", lambda e, sbank=sbank, s_h=s_h: e.tensor_add(s_h, P[sbank][:, 0:512], mask_sb[:]),
                  waits=[W(f"{C}sc{h}")] + ([W("mask")] if (q == 0 and c == 0 and h == 0) else []),
                  inc=("vec", 1), key=f"{C}mask{h}")
                b.setbank(sbank, W(f"{C}mask{h}"))
                for it in range(2):
                    E("vector", lambda e, s_h=s_h, it=it, par=par: e.reduce_max(
                        svec[:, 16 + 4 * par + it: 17 + 4 * par + it], s_h[:, it * 256:(it + 1) * 256],
                        axis=AX.X, negate=True),
                        waits=[W(f"{C}mask{h}")] if it == 0 else (), inc=("vec", 1), key=f"{C}max{h}{it}")
                for it in range(2):
                    E("scalar", lambda e, pn_h=pn_h, s_h=s_h, it=it, par=par: e.activation(
                        pn_h[:, it * 256:(it + 1) * 256], s_h[:, it * 256:(it + 1) * 256], AF.Exp,
                        bias=svec[:, 16 + 4 * par + it: 17 + 4 * par + it],
                        accum_out=ssc[:, 16 + 4 * par + it: 17 + 4 * par + it]),
                        waits=[W(f"{C}max{h}{it}")],
                        inc=("sc", 1) if it == 1 else None, key=f"{C}expa{h}" if it == 1 else None)
                E("scalar", lambda e, par=par: e.activation(
                    ssc[:, 18 + 4 * par: 20 + 4 * par], ssc[:, 16 + 4 * par: 18 + 4 * par], AF.Identity),
                    waits=[W(f"{C}expa{h}")], inc=("sc", 1), key=f"{C}sum{h}")

            def emit_mid(h):
                par = h % 2
                ptb = par
                pn_h = pn[:, par * 512:(par + 1) * 512]
                for it in range(2):
                    E("vector", lambda e, it=it, par=par: e.reciprocal(
                        svec[:, 18 + 4 * par + it: 19 + 4 * par + it], ssc[:, 18 + 4 * par + it: 19 + 4 * par + it]),
                        waits=[W(f"{C}sum{h}")] if it == 0 else (), inc=("vec", 1), key=f"{C}rcp{h}{it}")
                for it in range(2):
                    E("vector", lambda e, pn_h=pn_h, it=it, par=par: e.tensor_scalar(
                        pn_h[:, it * 256:(it + 1) * 256], pn_h[:, it * 256:(it + 1) * 256],
                        svec[:, 18 + 4 * par + it: 19 + 4 * par + it], None, ALU.mult),
                        waits=[W(f"{C}rcp{h}{it}")], inc=("vec", 1) if it == 1 else None,
                        key=f"{C}norm{h}" if it == 1 else None)
                for it in range(2):
                    for jt in range(2):
                        ws = []
                        if it == 0 and jt == 0:
                            ws = [W(f"{C}norm{h}")]
                            ws += [W(f"{C}ptev{h - 2}")] if h >= 2 else b.bankw(6 + ptb)
                            ws += [W("identb")] if (q == 0 and c == 0 and h < 2) else []
                        E("tensor", lambda e, ptb=ptb, it=it, jt=jt, pn_h=pn_h: e.transpose(
                            Pb[ptb][:, (2 * it + jt) * 128:(2 * it + jt + 1) * 128],
                            pn_h[:, it * 256 + jt * 128: it * 256 + jt * 128 + 128], identb_sb[:]),
                            waits=ws, inc=("pe", 1) if (it == 1 and jt == 1) else None,
                            key=f"{C}pt{h}" if (it == 1 and jt == 1) else None)

            def emit_back(h):
                par = h % 2
                dq = h // 2
                pvbank = 2 + par
                ptb = par
                E("vector", lambda e, ptb=ptb: e.tensor_copy(
                    pT[ptb].rearrange("p (jt it f) -> p jt it f", jt=2, it=2),
                    Pb[ptb][:, 0:512].rearrange("p (it jt f) -> p jt it f", it=2, jt=2)),
                    waits=[W(f"{C}pt{h}")], inc=("vec", 1), key=f"{C}ptev{h}")
                b.setbank(6 + ptb, W(f"{C}ptev{h}"))
                for jt in range(2):
                    ws = []
                    if jt == 0:
                        ws = [W(f"{C}ptev{h}"), W(f"{C}vev1{h // 8}")]
                        ws += [W(f"{C}atev{h - 2}")] if h >= 2 else b.bankw(pvbank)
                    E("tensor", lambda e, pvbank=pvbank, jt=jt, h=h, ptb=ptb: e.matmul(
                        P[pvbank][0:64, 0:256], v_sb[:, jt * 1024 + h * 64: jt * 1024 + h * 64 + 64],
                        pT[ptb][:, jt * 256:(jt + 1) * 256], start=(jt == 0), stop=(jt == 1)),
                        waits=ws, inc=("pe", 1) if jt == 1 else None, key=f"{C}pv{h}" if jt == 1 else None)
                E("scalar", lambda e, pvbank=pvbank, par=par, dq=dq: e.activation(
                    attnT[64 * par:64 * par + 64, dq * 256:(dq + 1) * 256], P[pvbank][0:64, 0:256], AF.Identity),
                    waits=[W(f"{C}pv{h}")], inc=("sc", 1), key=f"{C}atev{h}")
                b.setbank(pvbank, W(f"{C}atev{h}"))

            for hh in range(NH + 2):
                if hh + 2 < NH:
                    emit_scores(hh + 2)
                if hh < NH:
                    emit_front(hh)
                if 0 <= hh - 1 < NH:
                    emit_mid(hh - 1)
                if 0 <= hh - 2 < NH:
                    emit_back(hh - 2)

            if c + 1 < NCH:
                emit_ln1(c + 1, ln1pend=ln1pend)

            if PHA_DEPTH < 4:
                continue
            for tt2 in range(2):
                tt = 2 * c + tt2
                for hoh in range(2):
                    bank = 4 + hoh
                    for dc in range(NHC):
                        ws = [W(f"{C}atev{min(2 * dc + 1, NH - 1)}")]
                        if dc == 0:
                            ws += b.bankw(bank)
                            if c == 0 and tt2 == 0 and hoh == 0:
                                ws.append(W(f"{T}wo"))
                        E("tensor", lambda e, bank=bank, dc=dc, tt2=tt2, hoh=hoh: e.matmul(
                            P[bank][:, 0:512], attnT[:, dc * 256 + tt2 * 128: dc * 256 + tt2 * 128 + 128],
                            bigA[:, WO0 + dc * 1024 + hoh * 512: WO0 + dc * 1024 + (hoh + 1) * 512],
                            start=(dc == 0), stop=(dc == NHC - 1)),
                            waits=ws, inc=("pe", 1) if dc == NHC - 1 else None,
                            key=f"{C}wo{tt2}{hoh}" if dc == NHC - 1 else None)
                r2, nmr2, k_nmr2, k_r2, m2 = lnchain(
                    f"{C}l2t{tt2}", q, tt2, [(P[4][:, 0:512], 512), (P[5][:, 0:512], 512)],
                    src_waits_v=[W(f"{C}wo{tt2}0"), W(f"{C}wo{tt2}1")],
                    src_waits_s=[W(f"{C}wo{tt2}0"), W(f"{C}wo{tt2}1")], dump=dumpA, psum_src=True)
                ysl = y[:, tt * H:(tt + 1) * H]
                yaw = [W(k_nmr2)]
                if c == 0 and q > 0:
                    yaw.append(W(f"{C}x"))
                E("vector", lambda e, ysl=ysl, nmr2=nmr2: e.tensor_scalar(ysl, ysl, nmr2, None, ALU.add),
                  waits=yaw, inc=("vec", 1), key=f"{C}ya{tt2}")
                for hoh in range(2):
                    E("vector", lambda e, tt=tt, hoh=hoh, r2=r2: e.scalar_tensor_tensor(
                        y[:, tt * H + hoh * 512: tt * H + (hoh + 1) * 512], P[4 + hoh][:, 0:512], r2,
                        y[:, tt * H + hoh * 512: tt * H + (hoh + 1) * 512], ALU.mult, ALU.add),
                        waits=[W(f"{C}ya{tt2}")] if hoh == 0 else [W(f"{C}yr{tt2}0")],
                        inc=("vec", 1), key=f"{C}yr{tt2}{hoh}")
                    b.setbank(4 + hoh, W(f"{C}yr{tt2}{hoh}"))

        markA_pe = b.c.get("pe", 0)
        markA_vec = b.c.get("vec", 0)
        markA_sc = b.c.get("sc", 0)

        if not PHASEB:
            lastk = {1: f"{T}c{NCH-1}x1T11", 2: f"{T}c{NCH-1}vev11",
                     3: f"{T}c{NCH-1}atev{NH-1}", 4: f"{T}c{NCH-1}yr11"}[PHA_DEPTH]
            E("scalar", lambda e, q=q: e.dma_start(out=out4[q], in_=yv),
              waits=[W(lastk)], inc=("st", 16), key=f"store_q{q}")
            b.ev[f"s2end_q{q}"] = b.ev[lastk if PHA_DEPTH < 4 else f"{T}c{NCH-1}wo11"]
            continue
        ykey = f"{T}c{NCH-1}yr11" if PHASEA else f"{T}c{NCH-1}x"
        srcs8 = [y[:, tt * H:(tt + 1) * H] for tt in range(NTT)]
        wv8 = [[W(ykey)] if tt == 0 else [] for tt in range(NTT)]
        ws8 = [[W(ykey), ("abs", "pe", markA_pe), ("abs", "vec", markA_vec)] if tt == 0 else []
               for tt in range(NTT)]
        rres3 = ln_pipe8(f"{T}l3_", srcs8, wv8, ws8, dump3)
        for tt in range(NTT):
            B2 = f"{T}b{tt}"
            ysl = y[:, tt * H:(tt + 1) * H]
            x2b = x2buf if tt % 2 == 0 else x2buf2
            r3, nmr3, k_nmr3 = rres3[tt]
            E("scalar", lambda e, ysl=ysl, r3=r3, nmr3=nmr3, x2b=x2b: e.activation(
                x2b, ysl, AF.Identity, bias=nmr3, scale=r3),
                waits=[W(k_nmr3)] + ([W(f"{T}b{tt-2}x2tr1")] if tt >= 2 else
                                      [("abs", "pe", markA_pe), ("abs", "vec", markA_vec)]),
                inc=("sc", 1), key=f"{B2}x2")
            for g2 in range(2):
                bank = g2
                for k4 in range(4):
                    hc = g2 * 4 + k4
                    ws = []
                    if k4 == 0:
                        ws = [W(f"{B2}x2")] + b.bankw(bank)
                    E("tensor", lambda e, bank=bank, k4=k4, hc=hc, x2b=x2b: e.transpose(
                        P[bank][:, k4 * 128:(k4 + 1) * 128], x2b[:, hc * 128:(hc + 1) * 128], identf_sb[:]),
                        waits=ws, inc=("pe", 1) if k4 == 3 else None, key=f"{B2}x2tr{g2}" if k4 == 3 else None)
                E("vector", lambda e, g2=g2, tt=tt, bank=bank: e.tensor_copy(
                    x2Tv[:, g2 * 4:(g2 + 1) * 4, tt * 128:(tt + 1) * 128],
                    P[bank][:, 0:512].rearrange("p (k f) -> p k f", k=4)),
                    waits=[W(f"{B2}x2tr{g2}")], inc=("vec", 1), key=f"{B2}x2T{g2}")
                b.setbank(bank, W(f"{B2}x2T{g2}"))
            for hc in range(NHC):
                ws = []
                if hc == 0:
                    ws = [W(f"{B2}x2T1")] + b.bankw(2) + ([W("wrg")] if (q == 0 and tt == 0) else [])
                E("tensor", lambda e, hc=hc, tt=tt: e.matmul(
                    P[2][:, 0:16], x2T[:, hc * TQ + tt * 128: hc * TQ + tt * 128 + 128],
                    wrgb[:, hc * 16:(hc + 1) * 16], start=(hc == 0), stop=(hc == NHC - 1)),
                    waits=ws, inc=("pe", 1) if hc == NHC - 1 else None, key=f"{B2}rg" if hc == NHC - 1 else None)
            E("vector", lambda e: e.reduce_max(svec[:, 24:25], P[2][:, 0:8], axis=AX.X, negate=True),
              waits=[W(f"{B2}rg")], inc=("vec", 1), key=f"{B2}mxA")
            E("vector", lambda e: e.reduce_max(svec[:, 25:26], P[2][:, 8:16], axis=AX.X, negate=True),
              inc=("vec", 1), key=f"{B2}mxB")
            E("vector", lambda e: e.tensor_scalar(svec[:, 26:27], svec[:, 25:26], 10.0, None, ALU.mult),
              waits=[W(f"{B2}mxB")], inc=("vec", 1), key=f"{B2}mxB10")
            E("scalar", lambda e: e.activation(rg_sb[:, 0:8], P[2][:, 0:8], AF.Exp,
                                               bias=svec[:, 24:25], accum_out=ssc[:, 24:25]),
              waits=[W(f"{B2}mxA")])
            E("scalar", lambda e: e.activation(rg_sb[:, 8:16], P[2][:, 8:16], AF.Exp, scale=10.0,
                                               bias=svec[:, 26:27], accum_out=ssc[:, 25:26]),
              waits=[W(f"{B2}mxB10")], inc=("sc", 1), key=f"{B2}gexp")
            b.setbank(2, W(f"{B2}gexp"))
            E("scalar", lambda e: e.activation(ssc[:, 26:28], ssc[:, 24:26], AF.Identity),
              waits=[W(f"{B2}gexp")], inc=("sc", 1), key=f"{B2}sums")
            E("vector", lambda e: e.reciprocal(svec[:, 27:28], ssc[:, 26:27]),
              waits=[W(f"{B2}sums")], inc=("vec", 1), key=f"{B2}rca")
            E("vector", lambda e: e.reciprocal(svec[:, 28:29], ssc[:, 27:28]),
              inc=("vec", 1), key=f"{B2}rcb")
            E("vector", lambda e: e.tensor_scalar(sv8[:, 0:8], rg_sb[:, 0:8], svec[:, 27:28], None, ALU.mult),
              waits=[W(f"{B2}rca")], inc=("vec", 1), key=f"{B2}rp")
            E("vector", lambda e: e.tensor_scalar(sv8[:, 8:16], rg_sb[:, 8:16], svec[:, 28:29], None, ALU.mult),
              waits=[W(f"{B2}rcb")], inc=("vec", 1), key=f"{B2}gw")
            E("vector", lambda e: e.tensor_mul(sv8[:, 0:8], sv8[:, 0:8], sv8[:, 8:16]),
              waits=[W(f"{B2}rp"), W(f"{B2}gw")], inc=("vec", 1), key=f"{B2}p")
            E("vector", lambda e: e.reduce_sum(svec[:, 29:30], sv8[:, 0:8], axis=AX.X),
              waits=[W(f"{B2}p")], inc=("vec", 1), key=f"{B2}s3")
            E("vector", lambda e: e.tensor_scalar(svec[:, 30:31], svec[:, 29:30], 1e-9, None, ALU.add),
              waits=[W(f"{B2}s3")], inc=("vec", 1), key=f"{B2}s3b")
            E("vector", lambda e: e.reciprocal(svec[:, 31:32], svec[:, 30:31]),
              waits=[W(f"{B2}s3b")], inc=("vec", 1), key=f"{B2}s3r")
            E("vector", lambda e, tt=tt: e.tensor_scalar(cw_sb[:, tt * 8:(tt + 1) * 8], sv8[:, 0:8],
                                                         svec[:, 31:32], None, ALU.mult),
              waits=[W(f"{B2}s3r")], inc=("vec", 1), key=f"{B2}cw")

        wd_batch = 0
        for ee in range(NE):
            EB = f"{T}e{ee}"
            for fc in range(NFC):
                gidx = ee * NFC + fc
                s = gidx % 2
                fb = fc // 2
                gfb = (q * NE + ee) * NFB + fb
                fo = (fc % 2) * 128
                bufs = [wgb[gfb % NWB], wub[gfb % NWB]]
                for ui, (wbuf, gb0) in enumerate(((bufs[0], 4 * s), (bufs[1], 4 * s + 2))):
                    for hc in range(NHC):
                        for tb in range(2):
                            ws = []
                            if hc == 0 and tb == 0:
                                ws = [W(f"wgu{gfb}")] + b.bankw(gb0) + b.bankw(gb0 + 1)
                                if gidx == 0 and ui == 0:
                                    ws.append(W(f"{T}b{NTT-1}x2T1"))
                            E("tensor", lambda e, wbuf=wbuf, gb0=gb0, hc=hc, tb=tb, fo=fo: e.matmul(
                                P[gb0 + tb][:, 0:512],
                                wbuf[:, hc * FB + fo: hc * FB + fo + 128],
                                x2T[:, hc * TQ + tb * 512: hc * TQ + (tb + 1) * 512],
                                start=(hc == 0), stop=(hc == NHC - 1)),
                                waits=ws, inc=("pe", 1) if (hc == NHC - 1 and tb == 1) else None,
                                key=(f"{EB}s1g{fc}" if ui == 0 else f"{EB}s1u{fc}")
                                if (hc == NHC - 1 and tb == 1) else None)
                if fc % 2 == 1:
                    b.ev[f"s1fb{gfb}"] = b.ev[f"{EB}s1u{fc}"]
                for tb in range(2):
                    ws = [W(f"{EB}s1g{fc}")]
                    if gidx >= 2:
                        ws.append(W(f"{T}e{(gidx-2)//NFC}hm{(gidx-2)%NFC}_{tb}"))
                    E("scalar", lambda e, s=s, tb=tb: e.activation(sg[2 * s + tb], P[4 * s + tb][:, 0:512], AF.Silu),
                      waits=ws, inc=("sc", 1), key=f"{EB}silu{fc}_{tb}")
                    b.setbank(4 * s + tb, W(f"{EB}silu{fc}_{tb}"))
                for tb in range(2):
                    E("vector", lambda e, fc=fc, tb=tb, s=s: e.tensor_mul(
                        bigA[:, fc * 1024 + tb * 512: fc * 1024 + (tb + 1) * 512],
                        sg[2 * s + tb], P[4 * s + 2 + tb][:, 0:512]),
                        waits=[W(f"{EB}silu{fc}_{tb}"), W(f"{EB}s1u{fc}")],
                        inc=("vec", 1), key=f"{EB}hm{fc}_{tb}")
                    b.setbank(4 * s + 2 + tb, W(f"{EB}hm{fc}_{tb}"))
            for hdh in range(2):
                for fc in range(NFC):
                    kb = (q * NE + ee) * 2 * (NFC // 2) + hdh * (NFC // 2) + fc // 2
                    if fc % 2 == 0:
                        ws = []
                        if kb >= 2:
                            ws.append(W(f"wdcast{kb - 2}"))
                        if ee == 0 and hdh == 0 and fc == 0:
                            ws += [("abs", "sc", markA_sc), ("abs", "vec", markA_vec)]
                        src = d["wdown"][ee][fc * 128:(fc + 2) * 128, hdh * 512:(hdh + 1) * 512]
                        src3 = src.rearrange("(f2 p) n -> p f2 n", p=128)
                        E("sync", lambda e, kb=kb, src3=src3: e.dma_start(
                            out=wdst[kb % 2][:].rearrange("p (f2 n) -> p f2 n", f2=2), in_=src3),
                            waits=ws, inc=(f"wd{kb % 2}", 16), key=f"wddma{kb}")
                        ws = [W(f"wddma{kb}")]
                        if kb >= 2:
                            ws.append(W(f"s2f_{kb - 2}"))
                        E("scalar", lambda e, kb=kb: e.copy(
                            wdbf[:, (kb % 2) * 1024:(kb % 2 + 1) * 1024], wdst[kb % 2][:]),
                            waits=ws, inc=("sc", 1), key=f"wdcast{kb}")
                    slot = (kb % 2) * 1024 + (fc % 2) * 512
                    for tt in range(NTT):
                        ws = []
                        if tt == 0:
                            ws = [W(f"wdcast{kb}"), W(f"{EB}hm{fc}_1")]
                        if fc == 0:
                            ws = ws + b.bankw(tt)
                        E("tensor", lambda e, tt=tt, fc=fc, slot=slot: e.matmul(
                            P[tt][:, 0:512], bigA[:, fc * 1024 + tt * 128: fc * 1024 + tt * 128 + 128],
                            wdbf[:, slot:slot + 512], start=(fc == 0), stop=(fc == NFC - 1)),
                            waits=ws, inc=("pe", 1) if tt == NTT - 1 else None,
                            key=f"{EB}s2_{hdh}_{fc}" if tt == NTT - 1 else None)
                    if fc % 2 == 1:
                        b.ev[f"s2f_{kb}"] = b.ev[f"{EB}s2_{hdh}_{fc}"]
                for tt in range(NTT):
                    col = cw_sb[:, tt * 8 + ee: tt * 8 + ee + 1]
                    dst = accum[:, tt * H + hdh * 512: tt * H + (hdh + 1) * 512]
                    if ee == 0:
                        E("vector", lambda e, dst=dst, tt=tt, col=col: e.tensor_scalar(
                            dst, P[tt][:, 0:512], col, None, ALU.mult),
                            waits=[W(f"{EB}s2_{hdh}_{NFC-1}")], inc=("vec", 1), key=f"{EB}acc{hdh}_{tt}")
                    else:
                        E("vector", lambda e, dst=dst, tt=tt, col=col: e.scalar_tensor_tensor(
                            dst, P[tt][:, 0:512], col, dst, ALU.mult, ALU.add),
                            waits=[W(f"{EB}s2_{hdh}_{NFC-1}")], inc=("vec", 1), key=f"{EB}acc{hdh}_{tt}")
                    b.setbank(tt, W(f"{EB}acc{hdh}_{tt}"))
        b.ev[f"s2end_q{q}"] = b.ev[f"{T}e{NE-1}s2_1_{NFC-1}"]

        srcs4 = [accum[:, tt * H:(tt + 1) * H] for tt in range(NTT)]
        wv4 = [[W(f"{T}e{NE-1}acc1_{tt}")] for tt in range(NTT)]
        ws4 = [[W(f"{T}e{NE-1}acc1_{tt}")] for tt in range(NTT)]
        rres4 = ln_pipe8(f"{T}l4_", srcs4, wv4, ws4, dump3, pcol=24)

        if q + 1 < NQ and PHASEA and PHASEB and PHA_DEPTH >= 4:
            nq = q + 1
            CN = f"q{nq}c0"
            xw = [W(f"q{q}c0x1t0"), W(f"q{q}c0x1t1")] if q > 0 else []
            E("gpsimd", lambda e, nq=nq: e.dma_start(
                out=xsc[:].rearrange("p (tt h) -> p tt h", tt=2), in_=x5[nq, 0]),
                waits=xw, inc=("xsc", 16), key=f"xscq{nq}")
            res_early = lnchain2(
                f"{CN}l1t0", f"{CN}l1t1",
                xsc[:, 0:1024], xsc[:, 1024:2048],
                [W(f"xscq{nq}")],
                [W(f"xscq{nq}"), W(f"{T}e{NE-1}hm{NFC-2}_1")],
                [], [W(f"{T}e{NE-1}hm{NFC-1}_1")],
                (arena[:, 3072:4096], arena[:, 4096:5120]))
            for tt2 in range(2):
                x1b = x1buf if tt2 == 0 else dumpA
                if tt2 == 0:
                    lw = [W(f"{T}b{NTT-1}x2tr1")]
                else:
                    lw = [W(f"{T}c{NCH-1}pt{NH-1}"), W(f"{T}c{NCH-1}norm{NH-1}"),
                          W(f"{T}c{NCH-1}trg11")]
                r1, nmr1, k_nmr1 = res_early[f"{CN}l1t{tt2}"]
                src = xsc[:, tt2 * 1024:(tt2 + 1) * 1024]
                E("scalar", lambda e, src=src, r1=r1, nmr1=nmr1, x1b=x1b: e.activation(
                    x1b, src, AF.Identity, bias=nmr1, scale=r1),
                    waits=[W(k_nmr1)] + lw, inc=("sc", 1), key=f"{CN}x1t{tt2}")
        for tt in range(NTT):
            B4 = f"{T}o{tt}"
            atile = accum[:, tt * H:(tt + 1) * H]
            r4, nmr4, k_nmr4 = rres4[tt]
            ysl = y[:, tt * H:(tt + 1) * H]
            E("vector", lambda e, ysl=ysl, nmr4=nmr4: e.tensor_scalar(ysl, ysl, nmr4, None, ALU.add),
              waits=[W(k_nmr4)], inc=("vec", 1), key=f"{B4}ya")
            E("vector", lambda e, ysl=ysl, atile=atile, r4=r4: e.scalar_tensor_tensor(
                ysl, atile, r4, ysl, ALU.mult, ALU.add),
                waits=[W(f"{B4}ya")], inc=("vec", 1), key=f"{B4}out")
        for tt in range(NTT):
            stw = [W(f"{T}o{tt}out")]
            if q > 0 or tt > 0:
                stw.append(W(f"store_q{q}_tt{tt-1}" if tt > 0 else f"store_q{q-1}_tt{NTT-1}"))
            E("sync", lambda e, q=q, tt=tt: e.dma_start(out=out4[q][:, tt, :], in_=yv[:, tt, :]),
              waits=stw, inc=("st", 16), key=f"store_q{q}_tt{tt}")
        b.ev[f"store_q{q}"] = b.ev[f"store_q{q}_tt{NTT-1}"]

    E("sync", None, waits=[("abs", "st", NQ * NTT * 16)])

    def resolve(wspec):
        if wspec[0] == "ev":
            return b.ev[wspec[1]]
        if wspec[0] == "abs":
            return wspec[1], wspec[2]
        raise ValueError(wspec)

    with nc.Block() as block:
        def runner(ops):
            def f(eng):
                for waits, fn, inc in ops:
                    seen = {}
                    for wsp in waits:
                        sem, val = resolve(wsp)
                        if seen.get(sem, -1) >= val:
                            continue
                        seen[sem] = val
                        eng.wait_ge(sems[sem], val)
                    if fn is None:
                        continue
                    ins = fn(eng)
                    if inc:
                        ins.then_inc(sems[inc[0]], inc[1])
            return f
        block.sync(runner(b.ops["sync"]))
        block.gpsimd(runner(b.ops["gpsimd"]))
        block.tensor(runner(b.ops["tensor"]))
        block.vector(runner(b.ops["vector"]))
        block.scalar(runner(b.ops["scalar"]))

    ctx.close()
    return nc


def kernel(L0_attn_x, L0_attn_Wq, L0_attn_Wk, L0_attn_Wv, L0_attn_Wo,
           L0_ffn_w_router, L0_ffn_w_gating, L0_ffn_w_gate, L0_ffn_w_up, L0_ffn_w_down):
    x = np.ascontiguousarray(np.asarray(L0_attn_x, dtype=np.float32))
    consts = {
        "wq": np.ascontiguousarray(np.asarray(L0_attn_Wq, np.float32)),
        "wk": np.ascontiguousarray(np.asarray(L0_attn_Wk, np.float32)),
        "wv": np.ascontiguousarray(np.asarray(L0_attn_Wv, np.float32)),
        "wo": np.ascontiguousarray(np.asarray(L0_attn_Wo, np.float32)),
        "wr": np.ascontiguousarray(np.asarray(L0_ffn_w_router, np.float32)),
        "wgt": np.ascontiguousarray(np.asarray(L0_ffn_w_gating, np.float32)),
        "wgate": np.ascontiguousarray(np.asarray(L0_ffn_w_gate, np.float32)),
        "wup": np.ascontiguousarray(np.asarray(L0_ffn_w_up, np.float32)),
        "wdown": np.ascontiguousarray(np.asarray(L0_ffn_w_down, np.float32)),
        "identf": np.eye(128, dtype=np.float32),
        "identb": np.eye(128, dtype=np.float32).astype(ml_dtypes.bfloat16),
    }
    m = np.full((128, 512), MASKV, np.float32)
    jj = np.arange(256)
    for p in range(128):
        ok0 = (jj <= p) & (jj >= p - 64)
        ok1 = (jj <= 128 + p) & (jj >= 128 + p - 64)
        m[p, 0:256][ok0] = 0.0
        m[p, 256:512][ok1] = 0.0
    consts["mask"] = m

    nc = build_program()
    in_maps = []
    for core in range(NCORE):
        im = dict(consts)
        im["x"] = np.ascontiguousarray(x[core])
        in_maps.append(im)
    res = run_bass_kernel_spmd(nc, in_maps, core_ids=list(range(NCORE)))
    out = np.stack([res.results[core]["out"] for core in range(NCORE)], axis=0)
    return out


# version 18
# speedup vs baseline: 1.0735x; 1.0735x over previous
import numpy as np
import ml_dtypes
import concourse.bass as bass
import concourse.mybir as mybir
from concourse.bass_utils import run_bass_kernel_spmd
from contextlib import ExitStack

F32 = mybir.dt.float32
BF16 = mybir.dt.bfloat16
AF = mybir.ActivationFunctionType
ALU = mybir.AluOpType
AX = mybir.AxisListType

NCORE = 8
NQ = 4
TQ = 1024
NCH = 4
CH = 256
NH = 16
HD = 64
H = 1024
NHC = 8
NE = 8
FF = 4096
NFC = 32
NTT = 8
FB = 256
NFB = FF // FB
NWB = 2
EPS = 1e-5
MASKV = -1e30
QSCL = 0.125
PHASEA = True
PHASEB = True
PHA_DEPTH = 4

ENGS = ("sync", "gpsimd", "tensor", "vector", "scalar")


class Bld:
    def __init__(self):
        self.ops = {e: [] for e in ENGS}
        self.c = {}
        self.ev = {}
        self.psfree = {}

    def emit(self, eng, fn, waits=(), inc=None, key=None):
        if inc:
            sem, n = inc
            self.c[sem] = self.c.get(sem, 0) + n
            if key is not None:
                self.ev[key] = (sem, self.c[sem])
        self.ops[eng].append((list(waits), fn, inc))
        return self.c.get(inc[0]) if inc else None

    def w(self, key):
        return ("ev", key)

    def bankw(self, bank):
        return list(self.psfree.get(bank, []))

    def setbank(self, bank, key):
        self.psfree[bank] = [key]

    def addbank(self, bank, key):
        self.psfree.setdefault(bank, []).append(key)


def build_program():
    nc = bass.Bass("TRN2", target_bir_lowering=False)
    d = {}
    d["x"] = nc.dram_tensor("x", [NQ * TQ, H], F32, kind="ExternalInput").ap()
    for n, shp in (("wq", [H, H]), ("wk", [H, H]), ("wv", [H, H]), ("wo", [H, H]),
                   ("wr", [H, 8]), ("wgt", [H, 8]),
                   ("wgate", [NE, H, FF]), ("wup", [NE, H, FF]), ("wdown", [NE, FF, H]),
                   ("identf", [128, 128]), ("mask", [128, 512])):
        d[n] = nc.dram_tensor(n, shp, F32, kind="ExternalInput").ap()
    d["identb"] = nc.dram_tensor("identb", [128, 128], BF16, kind="ExternalInput").ap()
    out_d = nc.dram_tensor("out", [NQ * TQ, H], F32, kind="ExternalOutput").ap()

    ctx = ExitStack()
    sb = lambda name, shape, dt: ctx.enter_context(nc.sbuf_tensor(name, shape, dt))
    bigA = sb("bigA", [128, 32 * 1024], BF16)
    y = sb("y", [128, NTT * H], F32)
    accum = sb("accum", [128, NTT * H], F32)
    x2T = sb("x2T", [128, NHC * TQ], BF16)
    wgb = [sb(f"wgb{i}", [128, NHC * FB], BF16) for i in range(NWB)]
    wub = [sb(f"wub{i}", [128, NHC * FB], BF16) for i in range(NWB)]
    xsc = sb("xsc", [128, 2048], F32)
    arena = sb("arena", [128, 8704], F32)
    svec = sb("svec", [128, 48], F32)
    ssc = sb("ssc", [128, 48], F32)
    sv8 = sb("sv8", [128, 16], F32)
    rg_sb = sb("rg", [128, 16], F32)
    cw_sb = sb("cw", [128, NTT * 8], F32)
    wrgb = sb("wrgb", [128, NHC * 16], BF16)
    identf_sb = sb("identfs", [128, 128], F32)
    identb_sb = sb("identbs", [128, 128], BF16)
    mask_sb = sb("masks", [128, 512], F32)
    eps_sb = sb("epss", [128, 1], F32)

    arB = arena[:].bitcast(BF16)
    x1buf = arena[:, 0:1024]
    x2buf = arena[:, 0:1024]
    x2buf2 = arena[:, 7680:8704]
    dumpA = arena[:, 1024:2048]
    s_sb = arena[:, 2048:3072]
    dump3 = arena[:, 3072:4096]
    wdst = [arena[:, 1024:2048], arena[:, 2048:3072]]
    sg = [arena[:, 3072 + i * 512: 3072 + (i + 1) * 512] for i in range(4)]
    qT = arB[:, 6144:8192]
    kT = arB[:, 8192:10240]
    v_sb = arB[:, 10240:12288]
    wdbf = arB[:, 10240:12288]
    attnT = arB[:, 12288:14336]
    x1T = arB[:, 14336:16384]
    pT = [arB[:, 16384:16896], arB[:, 16896:17408]]
    pn = arB[:, 2048:4096]

    P = [ctx.enter_context(nc.psum_tensor(f"P{i}", [128, 512], F32)) for i in range(8)]
    Pb = [P[6][:].bitcast(BF16), P[7][:].bitcast(BF16)]

    sems = {}
    for s in ("cst", "wrg", "awq", "awk", "awv", "awo", "xs0", "xs1", "xsc",
              "wgu0", "wgu1", "wgu2", "wd0", "wd1", "wd2", "st", "pe", "vec", "sc"):
        sems[s] = ctx.enter_context(nc.semaphore(s))

    b = Bld()
    E = b.emit
    W = b.w

    WQ0, WK0, WV0, WO0 = 0, 8192, 16384, 24576

    E("vector", lambda e: e.memset(eps_sb[:], EPS), inc=("vec", 1), key="eps")
    E("sync", lambda e: e.dma_start(out=identf_sb[:], in_=d["identf"]), inc=("cst", 16), key="identf")
    E("sync", lambda e: e.dma_start(out=identb_sb[:], in_=d["identb"]), inc=("cst", 16), key="identb")
    E("sync", lambda e: e.dma_start(out=mask_sb[:], in_=d["mask"]), inc=("cst", 16), key="mask")
    wrg3 = wrgb[:].rearrange("p (hc e) -> p hc e", hc=NHC)
    E("gpsimd", lambda e: e.dma_start(
        out=wrg3[:, :, 0:8], in_=d["wr"].rearrange("(hc p) e -> p hc e", p=128)), inc=("wrg", 16))
    E("gpsimd", lambda e: e.dma_start(
        out=wrg3[:, :, 8:16], in_=d["wgt"].rearrange("(hc p) e -> p hc e", p=128)), inc=("wrg", 16), key="wrg")
    b.ev["identf"] = ("cst", 48)
    b.ev["identb"] = ("cst", 48)
    b.ev["mask"] = ("cst", 48)

    x5 = d["x"].rearrange("(q c tt p) h -> q c p tt h", q=NQ, c=NCH, tt=2, p=128)
    out4 = out_d.rearrange("(q tt p) h -> q p tt h", q=NQ, tt=NTT, p=128)
    yv = y[:].rearrange("p (tt h) -> p tt h", tt=NTT)
    x1Tv = x1T.rearrange("p (hc tt f) -> p hc tt f", hc=NHC, tt=2)
    x2Tv = x2T[:].rearrange("p (hc t) -> p hc t", hc=NHC)

    def lnchain(tag, q, s, srcs, src_waits_v, src_waits_s, dump, n=1024.0, extra_sq_waits=(), psum_src=False):
        sv, sss = s * 8, s * 8
        kk = None
        for i, (ap, w_) in enumerate(srcs):
            kk = E("vector", lambda e, ap=ap, i=i: e.reduce_sum(svec[:, sv + i:sv + i + 1], ap, axis=AX.X),
                   waits=src_waits_v if i == 0 else (), inc=("vec", 1), key=f"{tag}sum{i}")
        if len(srcs) == 2:
            E("vector", lambda e: e.tensor_add(svec[:, sv:sv + 1], svec[:, sv:sv + 1], svec[:, sv + 1:sv + 2]),
              waits=[W(f"{tag}sum1")], inc=("vec", 1), key=f"{tag}sumt")
        for i, (ap, w_) in enumerate(srcs):
            sqw = (list(src_waits_s) + list(extra_sq_waits)) if i == 0 else []
            if psum_src:
                sqw = sqw + [W(f"{tag}sum{i}")]
            E("scalar", lambda e, ap=ap, i=i, w_=w_: e.activation(
                dump[:, i * w_:(i + 1) * w_], ap, AF.Square, accum_out=ssc[:, sss + i:sss + i + 1]),
                waits=sqw,
                inc=("sc", 1) if i == len(srcs) - 1 else None, key=f"{tag}sqa" if i == len(srcs) - 1 else None)
        E("scalar", lambda e: e.activation(ssc[:, sss + 2:sss + 2 + len(srcs)], ssc[:, sss:sss + len(srcs)],
                                           AF.Identity), waits=[W(f"{tag}sqa")], inc=("sc", 1), key=f"{tag}sq")
        if len(srcs) == 2:
            E("vector", lambda e: e.tensor_add(svec[:, sv + 6:sv + 7], ssc[:, sss + 2:sss + 3],
                                               ssc[:, sss + 3:sss + 4]),
              waits=[W(f"{tag}sq")], inc=("vec", 1), key=f"{tag}sqt")
            sqsrc = svec[:, sv + 6:sv + 7]
        else:
            sqsrc = ssc[:, sss + 2:sss + 3]
        E("vector", lambda e: e.tensor_scalar(svec[:, sv + 2:sv + 3], svec[:, sv:sv + 1], 1.0 / n, None, ALU.mult),
          waits=[W(f"{tag}sumt" if len(srcs) == 2 else f"{tag}sum0")], inc=("vec", 1), key=f"{tag}m")
        E("vector", lambda e: e.tensor_mul(svec[:, sv + 3:sv + 4], svec[:, sv + 2:sv + 3], svec[:, sv + 2:sv + 3]),
          waits=[W(f"{tag}m")], inc=("vec", 1), key=f"{tag}m2")
        E("vector", lambda e, sqsrc=sqsrc: e.scalar_tensor_tensor(svec[:, sv + 4:sv + 5], sqsrc, 1.0 / n,
                                                     svec[:, sv + 3:sv + 4], ALU.mult, ALU.subtract),
          waits=[W(f"{tag}m2"), W(f"{tag}sqt" if len(srcs) == 2 else f"{tag}sq")],
          inc=("vec", 1), key=f"{tag}var")
        E("scalar", lambda e: e.activation(ssc[:, sss + 4:sss + 5], svec[:, sv + 4:sv + 5], AF.Ln,
                                           bias=eps_sb[:, 0:1]),
          waits=[W(f"{tag}var")], inc=("sc", 1), key=f"{tag}ln")
        E("scalar", lambda e: e.activation(ssc[:, sss + 5:sss + 6], ssc[:, sss + 4:sss + 5], AF.Exp, scale=-0.5),
          waits=[W(f"{tag}ln")], inc=("sc", 1), key=f"{tag}r")
        E("vector", lambda e: e.scalar_tensor_tensor(svec[:, sv + 5:sv + 6], svec[:, sv + 2:sv + 3], -1.0,
                                                     ssc[:, sss + 5:sss + 6], ALU.mult, ALU.mult),
          waits=[W(f"{tag}r")], inc=("vec", 1), key=f"{tag}nmr")
        return ssc[:, sss + 5:sss + 6], svec[:, sv + 5:sv + 6], f"{tag}nmr", f"{tag}r", svec[:, sv + 2:sv + 3]

    def lnchain2(tagA, tagB, srcA, srcB, wA, wsA, wB, wsB, dump2, lwA=(), lwB=(), base=32):
        out = {}
        for s2, (tag, src, wv, wsc) in enumerate(((tagA, srcA, wA, wsA), (tagB, srcB, wB, wsB))):
            E("vector", lambda e, src=src, s2=s2: e.reduce_sum(svec[:, base + s2 * 8:base + s2 * 8 + 1], src, axis=AX.X),
              waits=wv, inc=("vec", 1), key=f"{tag}sum0")
        for s2, (tag, src, wv, wsc) in enumerate(((tagA, srcA, wA, wsA), (tagB, srcB, wB, wsB))):
            ex = list(lwA) if s2 == 0 else list(lwB)
            E("scalar", lambda e, src=src, s2=s2: e.activation(
                dump2[s2], src, AF.Square,
                accum_out=ssc[:, base + s2 * 8:base + s2 * 8 + 1]),
                waits=list(wsc) + ex,
                inc=("sc", 1), key=f"{tag}sqa")
        for s2, tag in enumerate((tagA, tagB)):
            E("scalar", lambda e, s2=s2: e.activation(ssc[:, base + s2 * 8 + 2:base + s2 * 8 + 3], ssc[:, base + s2 * 8:base + s2 * 8 + 1],
                                                      AF.Identity), waits=[W(f"{tag}sqa")], inc=("sc", 1), key=f"{tag}sq")
        for s2, tag in enumerate((tagA, tagB)):
            E("vector", lambda e, s2=s2: e.tensor_scalar(svec[:, base + s2 * 8 + 2:base + s2 * 8 + 3], svec[:, base + s2 * 8:base + s2 * 8 + 1],
                                                         1.0 / 1024, None, ALU.mult),
              waits=[W(f"{tag}sum0")], inc=("vec", 1), key=f"{tag}m")
        for s2, tag in enumerate((tagA, tagB)):
            E("vector", lambda e, s2=s2: e.tensor_mul(svec[:, base + s2 * 8 + 3:base + s2 * 8 + 4], svec[:, base + s2 * 8 + 2:base + s2 * 8 + 3],
                                                      svec[:, base + s2 * 8 + 2:base + s2 * 8 + 3]),
              waits=[W(f"{tag}m")], inc=("vec", 1), key=f"{tag}m2")
        for s2, tag in enumerate((tagA, tagB)):
            E("vector", lambda e, s2=s2: e.scalar_tensor_tensor(svec[:, base + s2 * 8 + 4:base + s2 * 8 + 5],
                                                                ssc[:, base + s2 * 8 + 2:base + s2 * 8 + 3], 1.0 / 1024,
                                                                svec[:, base + s2 * 8 + 3:base + s2 * 8 + 4], ALU.mult, ALU.subtract),
              waits=[W(f"{tag}m2"), W(f"{tag}sq")], inc=("vec", 1), key=f"{tag}var")
        for s2, tag in enumerate((tagA, tagB)):
            E("scalar", lambda e, s2=s2: e.activation(ssc[:, base + s2 * 8 + 4:base + s2 * 8 + 5], svec[:, base + s2 * 8 + 4:base + s2 * 8 + 5],
                                                      AF.Ln, bias=eps_sb[:, 0:1]),
              waits=[W(f"{tag}var")], inc=("sc", 1), key=f"{tag}ln")
        for s2, tag in enumerate((tagA, tagB)):
            E("scalar", lambda e, s2=s2: e.activation(ssc[:, base + s2 * 8 + 5:base + s2 * 8 + 6], ssc[:, base + s2 * 8 + 4:base + s2 * 8 + 5],
                                                      AF.Exp, scale=-0.5),
              waits=[W(f"{tag}ln")], inc=("sc", 1), key=f"{tag}r")
        for s2, tag in enumerate((tagA, tagB)):
            E("vector", lambda e, s2=s2: e.scalar_tensor_tensor(svec[:, base + s2 * 8 + 5:base + s2 * 8 + 6],
                                                                svec[:, base + s2 * 8 + 2:base + s2 * 8 + 3], -1.0,
                                                                ssc[:, base + s2 * 8 + 5:base + s2 * 8 + 6], ALU.mult, ALU.mult),
              waits=[W(f"{tag}r")], inc=("vec", 1), key=f"{tag}nmr")
            out[tag] = (ssc[:, base + s2 * 8 + 5:base + s2 * 8 + 6], svec[:, base + s2 * 8 + 5:base + s2 * 8 + 6], f"{tag}nmr")
        return out

    def ln_pipe8(mktag, srcs8, wv8, ws8, dump_ap, pcol=0, coff=0, n8=8):
        res = {}
        R = range(n8)
        sC = lambda base, i: svec[:, base + coff + i: base + coff + i + 1]
        cC = lambda base, i: ssc[:, base + coff + i: base + coff + i + 1]
        for tt in R:
            E("vector", lambda e, tt=tt: e.reduce_sum(sC(8, tt), srcs8[tt], axis=AX.X),
              waits=wv8[tt], inc=("vec", 1), key=f"{mktag}{tt}sum")
        for tt in R:
            waw = [W(f"{mktag}{tt-1}sqa")] if tt > 0 else []
            E("scalar", lambda e, tt=tt: e.activation(dump_ap, srcs8[tt], AF.Square,
                                                      accum_out=cC(8, tt)),
              waits=list(ws8[tt]) + waw, inc=("sc", 1), key=f"{mktag}{tt}sqa")
        for tt in R:
            E("scalar", lambda e, tt=tt: e.activation(cC(16, tt), cC(8, tt), AF.Identity),
              waits=[W(f"{mktag}{tt}sqa")], inc=("sc", 1), key=f"{mktag}{tt}sq")
        for tt in R:
            E("vector", lambda e, tt=tt: e.tensor_scalar(sC(16, tt), sC(8, tt),
                                                         1.0 / 1024, None, ALU.mult),
              waits=[W(f"{mktag}{tt}sum")], inc=("vec", 1), key=f"{mktag}{tt}m")
        for tt in R:
            E("vector", lambda e, tt=tt: e.tensor_mul(sC(24, tt), sC(16, tt), sC(16, tt)),
              waits=[W(f"{mktag}{tt}m")], inc=("vec", 1), key=f"{mktag}{tt}m2")
        for tt in R:
            E("vector", lambda e, tt=tt: e.scalar_tensor_tensor(sC(24, tt), cC(16, tt),
                                                                1.0 / 1024, sC(24, tt),
                                                                ALU.mult, ALU.subtract),
              waits=[W(f"{mktag}{tt}m2"), W(f"{mktag}{tt}sq")], inc=("vec", 1), key=f"{mktag}{tt}var")
        for tt in R:
            E("scalar", lambda e, tt=tt: e.activation(cC(24, tt), sC(24, tt), AF.Ln,
                                                      bias=eps_sb[:, 0:1]),
              waits=[W(f"{mktag}{tt}var")], inc=("sc", 1), key=f"{mktag}{tt}ln")
        for tt in R:
            E("scalar", lambda e, tt=tt: e.activation(ssc[:, pcol + tt:pcol + tt + 1], cC(24, tt),
                                                      AF.Exp, scale=-0.5),
              waits=[W(f"{mktag}{tt}ln")], inc=("sc", 1), key=f"{mktag}{tt}r")
        for tt in R:
            E("vector", lambda e, tt=tt: e.scalar_tensor_tensor(svec[:, pcol + tt:pcol + tt + 1],
                                                                sC(16, tt), -1.0,
                                                                ssc[:, pcol + tt:pcol + tt + 1],
                                                                ALU.mult, ALU.mult),
              waits=[W(f"{mktag}{tt}r")], inc=("vec", 1), key=f"{mktag}{tt}nmr")
            res[tt] = (ssc[:, pcol + tt:pcol + tt + 1], svec[:, pcol + tt:pcol + tt + 1], f"{mktag}{tt}nmr")
        return res

    gfb_count = 0
    for q in range(NQ):
        T = f"q{q}"
        if q == 0:
            aw_waits = []
        elif PHASEB:
            aw_waits = [W(f"s2end_q{q-1}")]
        else:
            aw_waits = [W(f"q{q-1}c{NCH-1}wo11")]
        for nm, base, src, semn in (("wq", WQ0, d["wq"], "awq"), ("wk", WK0, d["wk"], "awk"),
                                    ("wv", WV0, d["wv"], "awv"), ("wo", WO0, d["wo"], "awo")):
            dst = bigA[:, base:base + 8192].rearrange("p (hc h2) -> p hc h2", hc=NHC)
            E("gpsimd", lambda e, dst=dst, src=src: e.dma_start(
                out=dst, in_=src.rearrange("(hc p) h2 -> p hc h2", p=128)),
                waits=aw_waits, inc=(semn, 16), key=f"{T}{nm}")
            aw_waits = []
        for ee in range(NE if PHASEB else 0):
            for fb in range(NFB):
                gfb = gfb_count
                gfb_count += 1
                ws = []
                if gfb >= NWB:
                    ws = [W(f"s1fb{gfb - NWB}")]
                bi = gfb % NWB
                for wsrc, wdstb in ((d["wgate"], wgb), (d["wup"], wub)):
                    src = wsrc[ee].rearrange("(hc p) f -> p hc f", p=128)[:, :, fb * FB:(fb + 1) * FB]
                    dstv = wdstb[bi][:].rearrange("p (hc f) -> p hc f", hc=NHC)
                    E("gpsimd", lambda e, dstv=dstv, src=src: e.dma_start(out=dstv, in_=src),
                      waits=ws, inc=(f"wgu{gfb % NWB}", 16), key=f"wgu{gfb}" if wsrc is d["wup"] else None)
                    ws = []

        if not PHASEA:
            for c in range(NCH):
                C = f"{T}c{c}"
                xw = [W(f"store_q{q-1}")] if (q > 0 and c == 0) else []
                E("sync", lambda e, q=q, c=c: e.dma_start(out=yv[:, 2 * c:2 * c + 2, :], in_=x5[q, c]),
                  waits=xw, inc=("xs0", 16), key=f"{C}x")
        for c in range(NCH if PHASEA else 0):
            C = f"{T}c{c}"
            xw = [W(f"store_q{q-1}_tt{2*c+1}")] if q > 0 else []
            gc = q * NCH + c
            if gc >= 2:
                pq, pc = divmod(gc - 2, NCH)
                xw += [W(f"q{pq}c{pc}x1t1"), W(f"q{pq}c{pc}l1t1sum0")]
            E("sync", lambda e, q=q, c=c: e.dma_start(out=yv[:, 2 * c:2 * c + 2, :], in_=x5[q, c]),
              waits=xw, inc=(f"xs{c % 2}", 16), key=f"{C}x")

            if c == 0:
                ln1pend = {}

                def emit_ln1(c2, q=q, T=T, ln1pend=None):
                    C2 = f"{T}c{c2}"
                    lwA, lwB = [], []
                    if c2 > 0:
                        if PHA_DEPTH >= 3:
                            lwA = [W(f"{T}c{c2-1}pt{NH-1}")]
                            lwB = [W(f"{T}c{c2-1}norm{NH-1}")]
                    elif q > 0:
                        if PHA_DEPTH >= 3:
                            lwA = [W(f"q{q-1}c{NCH-1}pt{NH-1}")]
                            lwB = [W(f"q{q-1}c{NCH-1}norm{NH-1}")]
                        if PHASEB:
                            kbl = q * NE * NFC * 2 // 2
                            lwA = lwA + [W(f"wddma{kbl-1}"), W(f"wddma{kbl-2}")]
                            lwB = lwB + [W(f"wddma{kbl-1}"), W(f"wddma{kbl-2}")]
                    ln1pend[c2] = lnchain2(
                        f"{C2}l1t0", f"{C2}l1t1",
                        y[:, (2 * c2) * H:(2 * c2 + 1) * H], y[:, (2 * c2 + 1) * H:(2 * c2 + 2) * H],
                        [W(f"{C2}x")], [W(f"{C2}x")], [], [],
                        (dumpA, s_sb), lwA=lwA, lwB=lwB)
                if q == 0 or not (PHASEA and PHASEB and PHA_DEPTH >= 4):
                    emit_ln1(0, ln1pend=ln1pend)
            early = (c == 0 and q > 0 and PHASEA and PHASEB and PHA_DEPTH >= 4)
            rres1 = None if early else ln1pend[c]
            for tt2 in range(2):
                tt = 2 * c + tt2
                ytile = y[:, tt * H:(tt + 1) * H]
                x1b = x1buf if tt2 == 0 else dumpA
                if not early:
                    lw_pe = []
                    if tt2 == 0:
                        if c > 0:
                            lw_pe = [W(f"{T}c{c-1}trg01")]
                        elif q > 0:
                            lw_pe = [W(f"q{q-1}b{NTT-1}x2tr1") if PHASEB else W(f"q{q-1}c{NCH-1}trg01")]
                    else:
                        if PHA_DEPTH >= 3:
                            if c > 0:
                                lw_pe = [W(f"{T}c{c-1}pt{NH-1}"), W(f"{T}c{c-1}norm{NH-1}")]
                            elif q > 0:
                                lw_pe = [W(f"q{q-1}c{NCH-1}pt{NH-1}"), W(f"q{q-1}c{NCH-1}norm{NH-1}")]
                        if c == 0 and q == 0:
                            lw_pe = []
                        if c > 0 or q > 0:
                            lw_pe = lw_pe + [W(f"{T}c{c-1}trg11") if c > 0 else
                                             (W(f"q{q-1}c{NCH-1}trg11") if PHASEA else W(f"q{q-1}c{NCH-1}x"))]
                    r1, nmr1, k_nmr1 = rres1[f"{C}l1t{tt2}"]
                    E("scalar", lambda e, ytile=ytile, r1=r1, nmr1=nmr1, x1b=x1b: e.activation(
                        x1b, ytile, AF.Identity, bias=nmr1, scale=r1),
                        waits=[W(k_nmr1)] + lw_pe, inc=("sc", 1), key=f"{C}x1t{tt2}")
                for g2 in range(2):
                    bank = g2
                    for k4 in range(4):
                        hc = g2 * 4 + k4
                        ws = []
                        if k4 == 0:
                            ws = [W(f"{C}x1t{tt2}")] + b.bankw(bank) + ([W("identf")] if (q == 0 and c == 0 and tt2 == 0) else [])
                        E("tensor", lambda e, bank=bank, k4=k4, hc=hc, x1b=x1b: e.transpose(
                            P[bank][:, k4 * 128:(k4 + 1) * 128], x1b[:, hc * 128:(hc + 1) * 128], identf_sb[:]),
                            waits=ws, inc=("pe", 1) if k4 == 3 else None, key=f"{C}trg{tt2}{g2}" if k4 == 3 else None)
                    if g2 == 1:
                        b.ev[f"{C}tr{tt2}"] = b.ev[f"{C}trg{tt2}{g2}"]
                    E("vector", lambda e, g2=g2, tt2=tt2, bank=bank: e.tensor_copy(
                        x1Tv[:, g2 * 4:(g2 + 1) * 4, tt2, :],
                        P[bank][:, 0:512].rearrange("p (k f) -> p k f", k=4)),
                        waits=[W(f"{C}trg{tt2}{g2}")], inc=("vec", 1), key=f"{C}x1T{tt2}{g2}")
                    b.setbank(bank, W(f"{C}x1T{tt2}{g2}"))

            if PHA_DEPTH < 2:
                if c + 1 < NCH:
                    emit_ln1(c + 1, ln1pend=ln1pend)
                continue
            gi = 0
            for proj in ("q", "k"):
                wbase = WQ0 if proj == "q" else WK0
                dstT = qT if proj == "q" else kT
                for dq in range(8):
                    bank = 4 + gi % 2
                    for hc in range(NHC):
                        ws = []
                        if hc == 0:
                            ws = [W(f"{C}x1T11")] + b.bankw(bank)
                            if c == 0 and gi == 0:
                                ws.append(W(f"{T}wq"))
                            if c == 0 and proj == "k" and dq == 0:
                                ws.append(W(f"{T}wk"))
                        E("tensor", lambda e, bank=bank, hc=hc, dq=dq, wbase=wbase: e.matmul(
                            P[bank][:, 0:256], bigA[:, wbase + hc * 1024 + dq * 128: wbase + hc * 1024 + (dq + 1) * 128],
                            x1T[:, hc * 256:(hc + 1) * 256], start=(hc == 0), stop=(hc == NHC - 1)),
                            waits=ws, inc=("pe", 1) if hc == NHC - 1 else None,
                            key=f"{C}prj{gi}" if hc == NHC - 1 else None)
                    if proj == "q":
                        E("scalar", lambda e, bank=bank, dq=dq: e.activation(
                            qT[:, dq * 256:(dq + 1) * 256], P[bank][:, 0:256], AF.Identity, scale=QSCL),
                            waits=[W(f"{C}prj{gi}")], inc=("sc", 1), key=f"{C}qev{dq}")
                        b.setbank(bank, W(f"{C}qev{dq}"))
                    else:
                        E("vector", lambda e, bank=bank, dq=dq: e.tensor_copy(
                            kT[:, dq * 256:(dq + 1) * 256], P[bank][:, 0:256]),
                            waits=[W(f"{C}prj{gi}")], inc=("vec", 1), key=f"{C}kev{dq}")
                        b.setbank(bank, W(f"{C}kev{dq}"))
                    gi += 1
            def emit_scores(h):
                par = h % 2
                dq = h // 2
                sbank = par
                bw = [W(f"{C}mask{h - 2}")] if h >= 2 else b.bankw(sbank)
                for it in range(2):
                    ws = []
                    if it == 0:
                        ws = [W(f"{C}qev{dq}"), W(f"{C}kev{dq}")] + bw
                    E("tensor", lambda e, sbank=sbank, it=it, par=par, dq=dq: e.matmul(
                        P[sbank][:, it * 256:(it + 1) * 256],
                        qT[64 * par:64 * par + 64, dq * 256 + it * 128: dq * 256 + it * 128 + 128],
                        kT[64 * par:64 * par + 64, dq * 256:(dq + 1) * 256],
                        start=True, stop=True),
                        waits=ws, inc=("pe", 1) if it == 1 else None, key=f"{C}sc{h}" if it == 1 else None)

            for tt2 in range(2):
                for dvh in range(2):
                    bank = 4 + gi % 2
                    for hc in range(NHC):
                        ws = []
                        if hc == 0:
                            ws = b.bankw(bank)
                            if c == 0 and tt2 == 0 and dvh == 0:
                                ws.append(W(f"{T}wv"))
                        E("tensor", lambda e, bank=bank, hc=hc, tt2=tt2, dvh=dvh: e.matmul(
                            P[bank][:, 0:512], x1T[:, hc * 256 + tt2 * 128: hc * 256 + tt2 * 128 + 128],
                            bigA[:, WV0 + hc * 1024 + dvh * 512: WV0 + hc * 1024 + (dvh + 1) * 512],
                            start=(hc == 0), stop=(hc == NHC - 1)),
                            waits=ws, inc=("pe", 1) if hc == NHC - 1 else None,
                            key=f"{C}prj{gi}" if hc == NHC - 1 else None)
                    E("vector", lambda e, bank=bank, tt2=tt2, dvh=dvh: e.tensor_copy(
                        v_sb[:, tt2 * 1024 + dvh * 512: tt2 * 1024 + (dvh + 1) * 512], P[bank][:, 0:512]),
                        waits=[W(f"{C}prj{gi}")], inc=("vec", 1), key=f"{C}vev{tt2}{dvh}")
                    b.setbank(bank, W(f"{C}vev{tt2}{dvh}"))
                    gi += 1

            if PHA_DEPTH < 3:
                if c + 1 < NCH:
                    emit_ln1(c + 1, ln1pend=ln1pend)
                continue
            emit_scores(0)
            emit_scores(1)

            def emit_front(h):
                par = h % 2
                sbank = par
                s_h = s_sb[:, par * 512:(par + 1) * 512]
                pn_h = pn[:, par * 512:(par + 1) * 512]
                E("vector", lambda e, sbank=sbank, s_h=s_h: e.tensor_add(s_h, P[sbank][:, 0:512], mask_sb[:]),
                  waits=[W(f"{C}sc{h}")] + ([W("mask")] if (q == 0 and c == 0 and h == 0) else []),
                  inc=("vec", 1), key=f"{C}mask{h}")
                b.setbank(sbank, W(f"{C}mask{h}"))
                for it in range(2):
                    E("vector", lambda e, s_h=s_h, it=it, par=par: e.reduce_max(
                        svec[:, 16 + 4 * par + it: 17 + 4 * par + it], s_h[:, it * 256:(it + 1) * 256],
                        axis=AX.X, negate=True),
                        waits=[W(f"{C}mask{h}")] if it == 0 else (), inc=("vec", 1), key=f"{C}max{h}{it}")
                for it in range(2):
                    E("scalar", lambda e, pn_h=pn_h, s_h=s_h, it=it, par=par: e.activation(
                        pn_h[:, it * 256:(it + 1) * 256], s_h[:, it * 256:(it + 1) * 256], AF.Exp,
                        bias=svec[:, 16 + 4 * par + it: 17 + 4 * par + it],
                        accum_out=ssc[:, 16 + 4 * par + it: 17 + 4 * par + it]),
                        waits=[W(f"{C}max{h}{it}")],
                        inc=("sc", 1) if it == 1 else None, key=f"{C}expa{h}" if it == 1 else None)
                E("scalar", lambda e, par=par: e.activation(
                    ssc[:, 18 + 4 * par: 20 + 4 * par], ssc[:, 16 + 4 * par: 18 + 4 * par], AF.Identity),
                    waits=[W(f"{C}expa{h}")], inc=("sc", 1), key=f"{C}sum{h}")

            def emit_mid(h):
                par = h % 2
                ptb = par
                pn_h = pn[:, par * 512:(par + 1) * 512]
                for it in range(2):
                    E("vector", lambda e, it=it, par=par: e.reciprocal(
                        svec[:, 18 + 4 * par + it: 19 + 4 * par + it], ssc[:, 18 + 4 * par + it: 19 + 4 * par + it]),
                        waits=[W(f"{C}sum{h}")] if it == 0 else (), inc=("vec", 1), key=f"{C}rcp{h}{it}")
                for it in range(2):
                    E("vector", lambda e, pn_h=pn_h, it=it, par=par: e.tensor_scalar(
                        pn_h[:, it * 256:(it + 1) * 256], pn_h[:, it * 256:(it + 1) * 256],
                        svec[:, 18 + 4 * par + it: 19 + 4 * par + it], None, ALU.mult),
                        waits=[W(f"{C}rcp{h}{it}")], inc=("vec", 1) if it == 1 else None,
                        key=f"{C}norm{h}" if it == 1 else None)
                for it in range(2):
                    for jt in range(2):
                        ws = []
                        if it == 0 and jt == 0:
                            ws = [W(f"{C}norm{h}")]
                            ws += [W(f"{C}ptev{h - 2}")] if h >= 2 else b.bankw(6 + ptb)
                            ws += [W("identb")] if (q == 0 and c == 0 and h < 2) else []
                        E("tensor", lambda e, ptb=ptb, it=it, jt=jt, pn_h=pn_h: e.transpose(
                            Pb[ptb][:, (2 * it + jt) * 128:(2 * it + jt + 1) * 128],
                            pn_h[:, it * 256 + jt * 128: it * 256 + jt * 128 + 128], identb_sb[:]),
                            waits=ws, inc=("pe", 1) if (it == 1 and jt == 1) else None,
                            key=f"{C}pt{h}" if (it == 1 and jt == 1) else None)

            def emit_back(h):
                par = h % 2
                dq = h // 2
                pvbank = 2 + par
                ptb = par
                E("vector", lambda e, ptb=ptb: e.tensor_copy(
                    pT[ptb].rearrange("p (jt it f) -> p jt it f", jt=2, it=2),
                    Pb[ptb][:, 0:512].rearrange("p (it jt f) -> p jt it f", it=2, jt=2)),
                    waits=[W(f"{C}pt{h}")], inc=("vec", 1), key=f"{C}ptev{h}")
                b.setbank(6 + ptb, W(f"{C}ptev{h}"))
                for jt in range(2):
                    ws = []
                    if jt == 0:
                        ws = [W(f"{C}ptev{h}"), W(f"{C}vev1{h // 8}")]
                        ws += [W(f"{C}atev{h - 2}")] if h >= 2 else b.bankw(pvbank)
                    E("tensor", lambda e, pvbank=pvbank, jt=jt, h=h, ptb=ptb: e.matmul(
                        P[pvbank][0:64, 0:256], v_sb[:, jt * 1024 + h * 64: jt * 1024 + h * 64 + 64],
                        pT[ptb][:, jt * 256:(jt + 1) * 256], start=(jt == 0), stop=(jt == 1)),
                        waits=ws, inc=("pe", 1) if jt == 1 else None, key=f"{C}pv{h}" if jt == 1 else None)
                E("scalar", lambda e, pvbank=pvbank, par=par, dq=dq: e.activation(
                    attnT[64 * par:64 * par + 64, dq * 256:(dq + 1) * 256], P[pvbank][0:64, 0:256], AF.Identity),
                    waits=[W(f"{C}pv{h}")], inc=("sc", 1), key=f"{C}atev{h}")
                b.setbank(pvbank, W(f"{C}atev{h}"))

            for hh in range(NH + 2):
                if hh + 2 < NH:
                    emit_scores(hh + 2)
                if hh < NH:
                    emit_front(hh)
                if 0 <= hh - 1 < NH:
                    emit_mid(hh - 1)
                if 0 <= hh - 2 < NH:
                    emit_back(hh - 2)

            if c + 1 < NCH:
                emit_ln1(c + 1, ln1pend=ln1pend)

            if PHA_DEPTH < 4:
                continue
            for tt2 in range(2):
                tt = 2 * c + tt2
                for hoh in range(2):
                    bank = 4 + hoh
                    for dc in range(NHC):
                        ws = [W(f"{C}atev{min(2 * dc + 1, NH - 1)}")]
                        if dc == 0:
                            ws += b.bankw(bank)
                            if c == 0 and tt2 == 0 and hoh == 0:
                                ws.append(W(f"{T}wo"))
                        E("tensor", lambda e, bank=bank, dc=dc, tt2=tt2, hoh=hoh: e.matmul(
                            P[bank][:, 0:512], attnT[:, dc * 256 + tt2 * 128: dc * 256 + tt2 * 128 + 128],
                            bigA[:, WO0 + dc * 1024 + hoh * 512: WO0 + dc * 1024 + (hoh + 1) * 512],
                            start=(dc == 0), stop=(dc == NHC - 1)),
                            waits=ws, inc=("pe", 1) if dc == NHC - 1 else None,
                            key=f"{C}wo{tt2}{hoh}" if dc == NHC - 1 else None)
                r2, nmr2, k_nmr2, k_r2, m2 = lnchain(
                    f"{C}l2t{tt2}", q, tt2, [(P[4][:, 0:512], 512), (P[5][:, 0:512], 512)],
                    src_waits_v=[W(f"{C}wo{tt2}0"), W(f"{C}wo{tt2}1")],
                    src_waits_s=[W(f"{C}wo{tt2}0"), W(f"{C}wo{tt2}1")], dump=dumpA, psum_src=True)
                ysl = y[:, tt * H:(tt + 1) * H]
                yaw = [W(k_nmr2)]
                if c == 0 and q > 0:
                    yaw.append(W(f"{C}x"))
                E("vector", lambda e, ysl=ysl, nmr2=nmr2: e.tensor_scalar(ysl, ysl, nmr2, None, ALU.add),
                  waits=yaw, inc=("vec", 1), key=f"{C}ya{tt2}")
                for hoh in range(2):
                    E("vector", lambda e, tt=tt, hoh=hoh, r2=r2: e.scalar_tensor_tensor(
                        y[:, tt * H + hoh * 512: tt * H + (hoh + 1) * 512], P[4 + hoh][:, 0:512], r2,
                        y[:, tt * H + hoh * 512: tt * H + (hoh + 1) * 512], ALU.mult, ALU.add),
                        waits=[W(f"{C}ya{tt2}")] if hoh == 0 else [W(f"{C}yr{tt2}0")],
                        inc=("vec", 1), key=f"{C}yr{tt2}{hoh}")
                    b.setbank(4 + hoh, W(f"{C}yr{tt2}{hoh}"))

        markA_pe = b.c.get("pe", 0)
        markA_vec = b.c.get("vec", 0)
        markA_sc = b.c.get("sc", 0)

        if not PHASEB:
            lastk = {1: f"{T}c{NCH-1}x1T11", 2: f"{T}c{NCH-1}vev11",
                     3: f"{T}c{NCH-1}atev{NH-1}", 4: f"{T}c{NCH-1}yr11"}[PHA_DEPTH]
            E("scalar", lambda e, q=q: e.dma_start(out=out4[q], in_=yv),
              waits=[W(lastk)], inc=("st", 16), key=f"store_q{q}")
            b.ev[f"s2end_q{q}"] = b.ev[lastk if PHA_DEPTH < 4 else f"{T}c{NCH-1}wo11"]
            continue
        srcs8 = [y[:, tt * H:(tt + 1) * H] for tt in range(NTT)]
        if PHASEA:
            wvA = [[W(f"{T}c0yr01")], [W(f"{T}c0yr11")]]
            wsA = [[W(f"{T}c0yr01"), ("abs", "pe", markA_pe), ("abs", "vec", markA_vec)], []]
            rA = ln_pipe8(f"{T}l3a_", srcs8[0:2], wvA, wsA, dump3, pcol=0, coff=0, n8=2)
            wvB = [[W(f"{T}c{(tt+2)//2}yr{tt%2}1")] for tt in range(6)]
            wsB = [[] for tt in range(6)]
            rB = ln_pipe8(f"{T}l3b_", srcs8[2:8], wvB, wsB, dump3, pcol=2, coff=2, n8=6)
            rres3 = {0: rA[0], 1: rA[1]}
            for i in range(6):
                rres3[i + 2] = rB[i]
        else:
            ykey = f"{T}c{NCH-1}x"
            wv8 = [[W(ykey)] if tt == 0 else [] for tt in range(NTT)]
            ws8 = [[W(ykey), ("abs", "pe", markA_pe), ("abs", "vec", markA_vec)] if tt == 0 else []
                   for tt in range(NTT)]
            rres3 = ln_pipe8(f"{T}l3_", srcs8, wv8, ws8, dump3)
        for tt in range(NTT):
            B2 = f"{T}b{tt}"
            ysl = y[:, tt * H:(tt + 1) * H]
            x2b = x2buf if tt % 2 == 0 else x2buf2
            r3, nmr3, k_nmr3 = rres3[tt]
            E("scalar", lambda e, ysl=ysl, r3=r3, nmr3=nmr3, x2b=x2b: e.activation(
                x2b, ysl, AF.Identity, bias=nmr3, scale=r3),
                waits=[W(k_nmr3)] + ([W(f"{T}b{tt-2}x2tr1")] if tt >= 2 else
                                      [("abs", "pe", markA_pe), ("abs", "vec", markA_vec)]),
                inc=("sc", 1), key=f"{B2}x2")
            for g2 in range(2):
                bank = g2
                for k4 in range(4):
                    hc = g2 * 4 + k4
                    ws = []
                    if k4 == 0:
                        ws = [W(f"{B2}x2")] + b.bankw(bank)
                    E("tensor", lambda e, bank=bank, k4=k4, hc=hc, x2b=x2b: e.transpose(
                        P[bank][:, k4 * 128:(k4 + 1) * 128], x2b[:, hc * 128:(hc + 1) * 128], identf_sb[:]),
                        waits=ws, inc=("pe", 1) if k4 == 3 else None, key=f"{B2}x2tr{g2}" if k4 == 3 else None)
                E("vector", lambda e, g2=g2, tt=tt, bank=bank: e.tensor_copy(
                    x2Tv[:, g2 * 4:(g2 + 1) * 4, tt * 128:(tt + 1) * 128],
                    P[bank][:, 0:512].rearrange("p (k f) -> p k f", k=4)),
                    waits=[W(f"{B2}x2tr{g2}")], inc=("vec", 1), key=f"{B2}x2T{g2}")
                b.setbank(bank, W(f"{B2}x2T{g2}"))
            for hc in range(NHC):
                ws = []
                if hc == 0:
                    ws = [W(f"{B2}x2T1")] + b.bankw(2) + ([W("wrg")] if (q == 0 and tt == 0) else [])
                E("tensor", lambda e, hc=hc, tt=tt: e.matmul(
                    P[2][:, 0:16], x2T[:, hc * TQ + tt * 128: hc * TQ + tt * 128 + 128],
                    wrgb[:, hc * 16:(hc + 1) * 16], start=(hc == 0), stop=(hc == NHC - 1)),
                    waits=ws, inc=("pe", 1) if hc == NHC - 1 else None, key=f"{B2}rg" if hc == NHC - 1 else None)
            E("vector", lambda e: e.reduce_max(svec[:, 24:25], P[2][:, 0:8], axis=AX.X, negate=True),
              waits=[W(f"{B2}rg")], inc=("vec", 1), key=f"{B2}mxA")
            E("vector", lambda e: e.reduce_max(svec[:, 25:26], P[2][:, 8:16], axis=AX.X, negate=True),
              inc=("vec", 1), key=f"{B2}mxB")
            E("vector", lambda e: e.tensor_scalar(svec[:, 26:27], svec[:, 25:26], 10.0, None, ALU.mult),
              waits=[W(f"{B2}mxB")], inc=("vec", 1), key=f"{B2}mxB10")
            E("scalar", lambda e: e.activation(rg_sb[:, 0:8], P[2][:, 0:8], AF.Exp,
                                               bias=svec[:, 24:25], accum_out=ssc[:, 24:25]),
              waits=[W(f"{B2}mxA")])
            E("scalar", lambda e: e.activation(rg_sb[:, 8:16], P[2][:, 8:16], AF.Exp, scale=10.0,
                                               bias=svec[:, 26:27], accum_out=ssc[:, 25:26]),
              waits=[W(f"{B2}mxB10")], inc=("sc", 1), key=f"{B2}gexp")
            b.setbank(2, W(f"{B2}gexp"))
            E("scalar", lambda e: e.activation(ssc[:, 26:28], ssc[:, 24:26], AF.Identity),
              waits=[W(f"{B2}gexp")], inc=("sc", 1), key=f"{B2}sums")
            E("vector", lambda e: e.reciprocal(svec[:, 27:28], ssc[:, 26:27]),
              waits=[W(f"{B2}sums")], inc=("vec", 1), key=f"{B2}rca")
            E("vector", lambda e: e.reciprocal(svec[:, 28:29], ssc[:, 27:28]),
              inc=("vec", 1), key=f"{B2}rcb")
            E("vector", lambda e: e.tensor_scalar(sv8[:, 0:8], rg_sb[:, 0:8], svec[:, 27:28], None, ALU.mult),
              waits=[W(f"{B2}rca")], inc=("vec", 1), key=f"{B2}rp")
            E("vector", lambda e: e.tensor_scalar(sv8[:, 8:16], rg_sb[:, 8:16], svec[:, 28:29], None, ALU.mult),
              waits=[W(f"{B2}rcb")], inc=("vec", 1), key=f"{B2}gw")
            E("vector", lambda e: e.tensor_mul(sv8[:, 0:8], sv8[:, 0:8], sv8[:, 8:16]),
              waits=[W(f"{B2}rp"), W(f"{B2}gw")], inc=("vec", 1), key=f"{B2}p")
            E("vector", lambda e: e.reduce_sum(svec[:, 29:30], sv8[:, 0:8], axis=AX.X),
              waits=[W(f"{B2}p")], inc=("vec", 1), key=f"{B2}s3")
            E("vector", lambda e: e.tensor_scalar(svec[:, 30:31], svec[:, 29:30], 1e-9, None, ALU.add),
              waits=[W(f"{B2}s3")], inc=("vec", 1), key=f"{B2}s3b")
            E("vector", lambda e: e.reciprocal(svec[:, 31:32], svec[:, 30:31]),
              waits=[W(f"{B2}s3b")], inc=("vec", 1), key=f"{B2}s3r")
            E("vector", lambda e, tt=tt: e.tensor_scalar(cw_sb[:, tt * 8:(tt + 1) * 8], sv8[:, 0:8],
                                                         svec[:, 31:32], None, ALU.mult),
              waits=[W(f"{B2}s3r")], inc=("vec", 1), key=f"{B2}cw")

        wd_batch = 0
        for ee in range(NE):
            EB = f"{T}e{ee}"
            for fc in range(NFC):
                gidx = ee * NFC + fc
                s = gidx % 2
                fb = fc // 2
                gfb = (q * NE + ee) * NFB + fb
                fo = (fc % 2) * 128
                bufs = [wgb[gfb % NWB], wub[gfb % NWB]]
                for ui, (wbuf, gb0) in enumerate(((bufs[0], 4 * s), (bufs[1], 4 * s + 2))):
                    for hc in range(NHC):
                        for tb in range(2):
                            ws = []
                            if hc == 0 and tb == 0:
                                ws = [W(f"wgu{gfb}")] + b.bankw(gb0) + b.bankw(gb0 + 1)
                                if gidx == 0 and ui == 0:
                                    ws.append(W(f"{T}b{NTT-1}x2T1"))
                            E("tensor", lambda e, wbuf=wbuf, gb0=gb0, hc=hc, tb=tb, fo=fo: e.matmul(
                                P[gb0 + tb][:, 0:512],
                                wbuf[:, hc * FB + fo: hc * FB + fo + 128],
                                x2T[:, hc * TQ + tb * 512: hc * TQ + (tb + 1) * 512],
                                start=(hc == 0), stop=(hc == NHC - 1)),
                                waits=ws, inc=("pe", 1) if (hc == NHC - 1 and tb == 1) else None,
                                key=(f"{EB}s1g{fc}" if ui == 0 else f"{EB}s1u{fc}")
                                if (hc == NHC - 1 and tb == 1) else None)
                if fc % 2 == 1:
                    b.ev[f"s1fb{gfb}"] = b.ev[f"{EB}s1u{fc}"]
                for tb in range(2):
                    ws = [W(f"{EB}s1g{fc}")]
                    if gidx >= 2:
                        ws.append(W(f"{T}e{(gidx-2)//NFC}hm{(gidx-2)%NFC}_{tb}"))
                    E("scalar", lambda e, s=s, tb=tb: e.activation(sg[2 * s + tb], P[4 * s + tb][:, 0:512], AF.Silu),
                      waits=ws, inc=("sc", 1), key=f"{EB}silu{fc}_{tb}")
                    b.setbank(4 * s + tb, W(f"{EB}silu{fc}_{tb}"))
                for tb in range(2):
                    E("vector", lambda e, fc=fc, tb=tb, s=s: e.tensor_mul(
                        bigA[:, fc * 1024 + tb * 512: fc * 1024 + (tb + 1) * 512],
                        sg[2 * s + tb], P[4 * s + 2 + tb][:, 0:512]),
                        waits=[W(f"{EB}silu{fc}_{tb}"), W(f"{EB}s1u{fc}")],
                        inc=("vec", 1), key=f"{EB}hm{fc}_{tb}")
                    b.setbank(4 * s + 2 + tb, W(f"{EB}hm{fc}_{tb}"))
            for hdh in range(2):
                for fc in range(NFC):
                    kb = (q * NE + ee) * 2 * (NFC // 2) + hdh * (NFC // 2) + fc // 2
                    if fc % 2 == 0:
                        ws = []
                        if kb >= 2:
                            ws.append(W(f"wdcast{kb - 2}"))
                        if ee == 0 and hdh == 0 and fc == 0:
                            ws += [("abs", "sc", markA_sc), ("abs", "vec", markA_vec)]
                        src = d["wdown"][ee][fc * 128:(fc + 2) * 128, hdh * 512:(hdh + 1) * 512]
                        src3 = src.rearrange("(f2 p) n -> p f2 n", p=128)
                        E("sync", lambda e, kb=kb, src3=src3: e.dma_start(
                            out=wdst[kb % 2][:].rearrange("p (f2 n) -> p f2 n", f2=2), in_=src3),
                            waits=ws, inc=(f"wd{kb % 2}", 16), key=f"wddma{kb}")
                        ws = [W(f"wddma{kb}")]
                        if kb >= 2:
                            ws.append(W(f"s2f_{kb - 2}"))
                        E("scalar", lambda e, kb=kb: e.copy(
                            wdbf[:, (kb % 2) * 1024:(kb % 2 + 1) * 1024], wdst[kb % 2][:]),
                            waits=ws, inc=("sc", 1), key=f"wdcast{kb}")
                    slot = (kb % 2) * 1024 + (fc % 2) * 512
                    for tt in range(NTT):
                        ws = []
                        if tt == 0:
                            ws = [W(f"wdcast{kb}"), W(f"{EB}hm{fc}_1")]
                        if fc == 0:
                            ws = ws + b.bankw(tt)
                        E("tensor", lambda e, tt=tt, fc=fc, slot=slot: e.matmul(
                            P[tt][:, 0:512], bigA[:, fc * 1024 + tt * 128: fc * 1024 + tt * 128 + 128],
                            wdbf[:, slot:slot + 512], start=(fc == 0), stop=(fc == NFC - 1)),
                            waits=ws, inc=("pe", 1) if tt == NTT - 1 else None,
                            key=f"{EB}s2_{hdh}_{fc}" if tt == NTT - 1 else None)
                    if fc % 2 == 1:
                        b.ev[f"s2f_{kb}"] = b.ev[f"{EB}s2_{hdh}_{fc}"]
                for tt in range(NTT):
                    col = cw_sb[:, tt * 8 + ee: tt * 8 + ee + 1]
                    dst = accum[:, tt * H + hdh * 512: tt * H + (hdh + 1) * 512]
                    if ee == 0:
                        E("vector", lambda e, dst=dst, tt=tt, col=col: e.tensor_scalar(
                            dst, P[tt][:, 0:512], col, None, ALU.mult),
                            waits=[W(f"{EB}s2_{hdh}_{NFC-1}")], inc=("vec", 1), key=f"{EB}acc{hdh}_{tt}")
                    else:
                        E("vector", lambda e, dst=dst, tt=tt, col=col: e.scalar_tensor_tensor(
                            dst, P[tt][:, 0:512], col, dst, ALU.mult, ALU.add),
                            waits=[W(f"{EB}s2_{hdh}_{NFC-1}")], inc=("vec", 1), key=f"{EB}acc{hdh}_{tt}")
                    b.setbank(tt, W(f"{EB}acc{hdh}_{tt}"))
        b.ev[f"s2end_q{q}"] = b.ev[f"{T}e{NE-1}s2_1_{NFC-1}"]

        if q + 1 < NQ and PHASEA and PHASEB and PHA_DEPTH >= 4:
            nq = q + 1
            CN = f"q{nq}c0"
            xw = [W(f"q{q}c0x1t0"), W(f"q{q}c0x1t1")] if q > 0 else []
            E("gpsimd", lambda e, nq=nq: e.dma_start(
                out=xsc[:].rearrange("p (tt h) -> p tt h", tt=2), in_=x5[nq, 0]),
                waits=xw, inc=("xsc", 16), key=f"xscq{nq}")
            res_early = lnchain2(
                f"{CN}l1t0", f"{CN}l1t1",
                xsc[:, 0:1024], xsc[:, 1024:2048],
                [W(f"xscq{nq}")],
                [W(f"xscq{nq}"), W(f"{T}e{NE-1}hm{NFC-2}_1")],
                [], [W(f"{T}e{NE-1}hm{NFC-1}_1")],
                (arena[:, 3072:4096], arena[:, 4096:5120]))
            for tt2 in range(2):
                x1b = x1buf if tt2 == 0 else dumpA
                if tt2 == 0:
                    lw = [W(f"{T}b{NTT-1}x2tr1")]
                else:
                    lw = [W(f"{T}c{NCH-1}pt{NH-1}"), W(f"{T}c{NCH-1}norm{NH-1}"),
                          W(f"{T}c{NCH-1}trg11")]
                r1, nmr1, k_nmr1 = res_early[f"{CN}l1t{tt2}"]
                src = xsc[:, tt2 * 1024:(tt2 + 1) * 1024]
                E("scalar", lambda e, src=src, r1=r1, nmr1=nmr1, x1b=x1b: e.activation(
                    x1b, src, AF.Identity, bias=nmr1, scale=r1),
                    waits=[W(k_nmr1)] + lw, inc=("sc", 1), key=f"{CN}x1t{tt2}")

        srcs4 = [accum[:, tt * H:(tt + 1) * H] for tt in range(NTT)]
        wv4 = [[W(f"{T}e{NE-1}acc1_{tt}")] for tt in range(NTT)]
        ws4 = [[W(f"{T}e{NE-1}acc1_{tt}")] for tt in range(NTT)]
        rres4 = ln_pipe8(f"{T}l4_", srcs4, wv4, ws4, dump3, pcol=24)

        for tt in range(NTT):
            B4 = f"{T}o{tt}"
            atile = accum[:, tt * H:(tt + 1) * H]
            r4, nmr4, k_nmr4 = rres4[tt]
            ysl = y[:, tt * H:(tt + 1) * H]
            E("vector", lambda e, ysl=ysl, nmr4=nmr4: e.tensor_scalar(ysl, ysl, nmr4, None, ALU.add),
              waits=[W(k_nmr4)], inc=("vec", 1), key=f"{B4}ya")
            E("vector", lambda e, ysl=ysl, atile=atile, r4=r4: e.scalar_tensor_tensor(
                ysl, atile, r4, ysl, ALU.mult, ALU.add),
                waits=[W(f"{B4}ya")], inc=("vec", 1), key=f"{B4}out")
        for tt in range(NTT):
            stw = [W(f"{T}o{tt}out")]
            if q > 0 or tt > 0:
                stw.append(W(f"store_q{q}_tt{tt-1}" if tt > 0 else f"store_q{q-1}_tt{NTT-1}"))
            E("sync", lambda e, q=q, tt=tt: e.dma_start(out=out4[q][:, tt, :], in_=yv[:, tt, :]),
              waits=stw, inc=("st", 16), key=f"store_q{q}_tt{tt}")
        b.ev[f"store_q{q}"] = b.ev[f"store_q{q}_tt{NTT-1}"]

    E("sync", None, waits=[("abs", "st", NQ * NTT * 16)])

    def resolve(wspec):
        if wspec[0] == "ev":
            return b.ev[wspec[1]]
        if wspec[0] == "abs":
            return wspec[1], wspec[2]
        raise ValueError(wspec)

    with nc.Block() as block:
        def runner(ops):
            def f(eng):
                for waits, fn, inc in ops:
                    seen = {}
                    for wsp in waits:
                        sem, val = resolve(wsp)
                        if seen.get(sem, -1) >= val:
                            continue
                        seen[sem] = val
                        eng.wait_ge(sems[sem], val)
                    if fn is None:
                        continue
                    ins = fn(eng)
                    if inc:
                        ins.then_inc(sems[inc[0]], inc[1])
            return f
        block.sync(runner(b.ops["sync"]))
        block.gpsimd(runner(b.ops["gpsimd"]))
        block.tensor(runner(b.ops["tensor"]))
        block.vector(runner(b.ops["vector"]))
        block.scalar(runner(b.ops["scalar"]))

    ctx.close()
    return nc


def kernel(L0_attn_x, L0_attn_Wq, L0_attn_Wk, L0_attn_Wv, L0_attn_Wo,
           L0_ffn_w_router, L0_ffn_w_gating, L0_ffn_w_gate, L0_ffn_w_up, L0_ffn_w_down):
    x = np.ascontiguousarray(np.asarray(L0_attn_x, dtype=np.float32))
    consts = {
        "wq": np.ascontiguousarray(np.asarray(L0_attn_Wq, np.float32)),
        "wk": np.ascontiguousarray(np.asarray(L0_attn_Wk, np.float32)),
        "wv": np.ascontiguousarray(np.asarray(L0_attn_Wv, np.float32)),
        "wo": np.ascontiguousarray(np.asarray(L0_attn_Wo, np.float32)),
        "wr": np.ascontiguousarray(np.asarray(L0_ffn_w_router, np.float32)),
        "wgt": np.ascontiguousarray(np.asarray(L0_ffn_w_gating, np.float32)),
        "wgate": np.ascontiguousarray(np.asarray(L0_ffn_w_gate, np.float32)),
        "wup": np.ascontiguousarray(np.asarray(L0_ffn_w_up, np.float32)),
        "wdown": np.ascontiguousarray(np.asarray(L0_ffn_w_down, np.float32)),
        "identf": np.eye(128, dtype=np.float32),
        "identb": np.eye(128, dtype=np.float32).astype(ml_dtypes.bfloat16),
    }
    m = np.full((128, 512), MASKV, np.float32)
    jj = np.arange(256)
    for p in range(128):
        ok0 = (jj <= p) & (jj >= p - 64)
        ok1 = (jj <= 128 + p) & (jj >= 128 + p - 64)
        m[p, 0:256][ok0] = 0.0
        m[p, 256:512][ok1] = 0.0
    consts["mask"] = m

    nc = build_program()
    in_maps = []
    for core in range(NCORE):
        im = dict(consts)
        im["x"] = np.ascontiguousarray(x[core])
        in_maps.append(im)
    res = run_bass_kernel_spmd(nc, in_maps, core_ids=list(range(NCORE)))
    out = np.stack([res.results[core]["out"] for core in range(NCORE)], axis=0)
    return out


# version 19
# speedup vs baseline: 1.1438x; 1.0656x over previous
import numpy as np
import ml_dtypes
import concourse.bass as bass
import concourse.mybir as mybir
from concourse.bass_utils import run_bass_kernel_spmd
from contextlib import ExitStack

F32 = mybir.dt.float32
BF16 = mybir.dt.bfloat16
AF = mybir.ActivationFunctionType
ALU = mybir.AluOpType
AX = mybir.AxisListType

NCORE = 8
NQ = 4
TQ = 1024
NCH = 4
CH = 256
NH = 16
HD = 64
H = 1024
NHC = 8
NE = 8
FF = 4096
NFC = 32
NTT = 8
FB = 256
NFB = FF // FB
NWB = 2
EPS = 1e-5
MASKV = -1e30
QSCL = 0.125
PHASEA = True
PHASEB = True
PHA_DEPTH = 4

ENGS = ("sync", "gpsimd", "tensor", "vector", "scalar")


class Bld:
    def __init__(self):
        self.ops = {e: [] for e in ENGS}
        self.c = {}
        self.ev = {}
        self.psfree = {}

    def emit(self, eng, fn, waits=(), inc=None, key=None):
        if inc:
            sem, n = inc
            self.c[sem] = self.c.get(sem, 0) + n
            if key is not None:
                self.ev[key] = (sem, self.c[sem])
        self.ops[eng].append((list(waits), fn, inc))
        return self.c.get(inc[0]) if inc else None

    def w(self, key):
        return ("ev", key)

    def bankw(self, bank):
        return list(self.psfree.get(bank, []))

    def setbank(self, bank, key):
        self.psfree[bank] = [key]

    def addbank(self, bank, key):
        self.psfree.setdefault(bank, []).append(key)


def build_program():
    nc = bass.Bass("TRN2", target_bir_lowering=False)
    d = {}
    d["x"] = nc.dram_tensor("x", [NQ * TQ, H], F32, kind="ExternalInput").ap()
    for n, shp in (("wq", [H, H]), ("wk", [H, H]), ("wv", [H, H]), ("wo", [H, H]),
                   ("wr", [H, 8]), ("wgt", [H, 8]),
                   ("wgate", [NE, H, FF]), ("wup", [NE, H, FF]), ("wdown", [NE, FF, H]),
                   ("identf", [128, 128]), ("mask", [128, 512])):
        d[n] = nc.dram_tensor(n, shp, F32, kind="ExternalInput").ap()
    d["identb"] = nc.dram_tensor("identb", [128, 128], BF16, kind="ExternalInput").ap()
    out_d = nc.dram_tensor("out", [NQ * TQ, H], F32, kind="ExternalOutput").ap()

    ctx = ExitStack()
    sb = lambda name, shape, dt: ctx.enter_context(nc.sbuf_tensor(name, shape, dt))
    bigA = sb("bigA", [128, 32 * 1024], BF16)
    y = sb("y", [128, NTT * H], F32)
    accum = sb("accum", [128, NTT * H], F32)
    x2T = sb("x2T", [128, NHC * TQ], BF16)
    wgb = [sb(f"wgb{i}", [128, NHC * FB], BF16) for i in range(NWB)]
    wub = [sb(f"wub{i}", [128, NHC * FB], BF16) for i in range(NWB)]
    xsc = sb("xsc", [128, 2048], F32)
    arena = sb("arena", [128, 8704], F32)
    svec = sb("svec", [128, 48], F32)
    ssc = sb("ssc", [128, 48], F32)
    sv8 = sb("sv8", [128, 16], F32)
    rg_sb = sb("rg", [128, 16], F32)
    cw_sb = sb("cw", [128, NTT * 8], F32)
    wrgb = sb("wrgb", [128, NHC * 16], BF16)
    identf_sb = sb("identfs", [128, 128], F32)
    identb_sb = sb("identbs", [128, 128], BF16)
    mask_sb = sb("masks", [128, 512], F32)
    eps_sb = sb("epss", [128, 1], F32)

    arB = arena[:].bitcast(BF16)
    x1buf = arena[:, 0:1024]
    x2buf = arena[:, 0:1024]
    x2buf2 = arena[:, 7680:8704]
    dumpA = arena[:, 1024:2048]
    s_sb = arena[:, 2048:3072]
    dump3 = arena[:, 3072:4096]
    wdst = [arena[:, 1024:2048], arena[:, 2048:3072]]
    sg = [arena[:, 3072 + i * 512: 3072 + (i + 1) * 512] for i in range(4)]
    qT = arB[:, 6144:8192]
    kT = arB[:, 8192:10240]
    v_sb = arB[:, 10240:12288]
    wdbf = arB[:, 10240:12288]
    attnT = arB[:, 12288:14336]
    x1T = arB[:, 14336:16384]
    pT = [arB[:, 16384:16896], arB[:, 16896:17408]]
    pn = arB[:, 2048:4096]

    P = [ctx.enter_context(nc.psum_tensor(f"P{i}", [128, 512], F32)) for i in range(8)]
    Pb = [P[6][:].bitcast(BF16), P[7][:].bitcast(BF16)]

    sems = {}
    for s in ("cst", "wrg", "awq", "awk", "awv", "awo", "xs0", "xs1", "xsc",
              "wgu0", "wgu1", "wgu2", "wd0", "wd1", "wd2", "st", "pe", "vec", "sc"):
        sems[s] = ctx.enter_context(nc.semaphore(s))

    b = Bld()
    E = b.emit
    W = b.w

    WQ0, WK0, WV0, WO0 = 0, 8192, 16384, 24576

    E("vector", lambda e: e.memset(eps_sb[:], EPS), inc=("vec", 1), key="eps")
    E("sync", lambda e: e.dma_start(out=identf_sb[:], in_=d["identf"]), inc=("cst", 16), key="identf")
    E("sync", lambda e: e.dma_start(out=identb_sb[:], in_=d["identb"]), inc=("cst", 16), key="identb")
    E("sync", lambda e: e.dma_start(out=mask_sb[:], in_=d["mask"]), inc=("cst", 16), key="mask")
    wrg3 = wrgb[:].rearrange("p (hc e) -> p hc e", hc=NHC)
    E("gpsimd", lambda e: e.dma_start(
        out=wrg3[:, :, 0:8], in_=d["wr"].rearrange("(hc p) e -> p hc e", p=128)), inc=("wrg", 16))
    E("gpsimd", lambda e: e.dma_start(
        out=wrg3[:, :, 8:16], in_=d["wgt"].rearrange("(hc p) e -> p hc e", p=128)), inc=("wrg", 16), key="wrg")
    b.ev["identf"] = ("cst", 48)
    b.ev["identb"] = ("cst", 48)
    b.ev["mask"] = ("cst", 48)

    x5 = d["x"].rearrange("(q c tt p) h -> q c p tt h", q=NQ, c=NCH, tt=2, p=128)
    out4 = out_d.rearrange("(q tt p) h -> q p tt h", q=NQ, tt=NTT, p=128)
    yv = y[:].rearrange("p (tt h) -> p tt h", tt=NTT)
    x1Tv = x1T.rearrange("p (hc tt f) -> p hc tt f", hc=NHC, tt=2)
    x2Tv = x2T[:].rearrange("p (hc t) -> p hc t", hc=NHC)

    def lnchain(tag, q, s, srcs, src_waits_v, src_waits_s, dump, n=1024.0, extra_sq_waits=(), psum_src=False):
        sv, sss = s * 8, s * 8
        kk = None
        for i, (ap, w_) in enumerate(srcs):
            kk = E("vector", lambda e, ap=ap, i=i: e.reduce_sum(svec[:, sv + i:sv + i + 1], ap, axis=AX.X),
                   waits=src_waits_v if i == 0 else (), inc=("vec", 1), key=f"{tag}sum{i}")
        if len(srcs) == 2:
            E("vector", lambda e: e.tensor_add(svec[:, sv:sv + 1], svec[:, sv:sv + 1], svec[:, sv + 1:sv + 2]),
              waits=[W(f"{tag}sum1")], inc=("vec", 1), key=f"{tag}sumt")
        for i, (ap, w_) in enumerate(srcs):
            sqw = (list(src_waits_s) + list(extra_sq_waits)) if i == 0 else []
            if psum_src:
                sqw = sqw + [W(f"{tag}sum{i}")]
            E("scalar", lambda e, ap=ap, i=i, w_=w_: e.activation(
                dump[:, i * w_:(i + 1) * w_], ap, AF.Square, accum_out=ssc[:, sss + i:sss + i + 1]),
                waits=sqw,
                inc=("sc", 1) if i == len(srcs) - 1 else None, key=f"{tag}sqa" if i == len(srcs) - 1 else None)
        E("scalar", lambda e: e.activation(ssc[:, sss + 2:sss + 2 + len(srcs)], ssc[:, sss:sss + len(srcs)],
                                           AF.Identity), waits=[W(f"{tag}sqa")], inc=("sc", 1), key=f"{tag}sq")
        if len(srcs) == 2:
            E("vector", lambda e: e.tensor_add(svec[:, sv + 6:sv + 7], ssc[:, sss + 2:sss + 3],
                                               ssc[:, sss + 3:sss + 4]),
              waits=[W(f"{tag}sq")], inc=("vec", 1), key=f"{tag}sqt")
            sqsrc = svec[:, sv + 6:sv + 7]
        else:
            sqsrc = ssc[:, sss + 2:sss + 3]
        E("vector", lambda e: e.tensor_scalar(svec[:, sv + 2:sv + 3], svec[:, sv:sv + 1], 1.0 / n, None, ALU.mult),
          waits=[W(f"{tag}sumt" if len(srcs) == 2 else f"{tag}sum0")], inc=("vec", 1), key=f"{tag}m")
        E("vector", lambda e: e.tensor_mul(svec[:, sv + 3:sv + 4], svec[:, sv + 2:sv + 3], svec[:, sv + 2:sv + 3]),
          waits=[W(f"{tag}m")], inc=("vec", 1), key=f"{tag}m2")
        E("vector", lambda e, sqsrc=sqsrc: e.scalar_tensor_tensor(svec[:, sv + 4:sv + 5], sqsrc, 1.0 / n,
                                                     svec[:, sv + 3:sv + 4], ALU.mult, ALU.subtract),
          waits=[W(f"{tag}m2"), W(f"{tag}sqt" if len(srcs) == 2 else f"{tag}sq")],
          inc=("vec", 1), key=f"{tag}var")
        E("scalar", lambda e: e.activation(ssc[:, sss + 4:sss + 5], svec[:, sv + 4:sv + 5], AF.Ln,
                                           bias=eps_sb[:, 0:1]),
          waits=[W(f"{tag}var")], inc=("sc", 1), key=f"{tag}ln")
        E("scalar", lambda e: e.activation(ssc[:, sss + 5:sss + 6], ssc[:, sss + 4:sss + 5], AF.Exp, scale=-0.5),
          waits=[W(f"{tag}ln")], inc=("sc", 1), key=f"{tag}r")
        E("vector", lambda e: e.scalar_tensor_tensor(svec[:, sv + 5:sv + 6], svec[:, sv + 2:sv + 3], -1.0,
                                                     ssc[:, sss + 5:sss + 6], ALU.mult, ALU.mult),
          waits=[W(f"{tag}r")], inc=("vec", 1), key=f"{tag}nmr")
        return ssc[:, sss + 5:sss + 6], svec[:, sv + 5:sv + 6], f"{tag}nmr", f"{tag}r", svec[:, sv + 2:sv + 3]

    def lnchain2(tagA, tagB, srcA, srcB, wA, wsA, wB, wsB, dump2, lwA=(), lwB=(), base=32):
        out = {}
        for s2, (tag, src, wv, wsc) in enumerate(((tagA, srcA, wA, wsA), (tagB, srcB, wB, wsB))):
            E("vector", lambda e, src=src, s2=s2: e.reduce_sum(svec[:, base + s2 * 8:base + s2 * 8 + 1], src, axis=AX.X),
              waits=wv, inc=("vec", 1), key=f"{tag}sum0")
        for s2, (tag, src, wv, wsc) in enumerate(((tagA, srcA, wA, wsA), (tagB, srcB, wB, wsB))):
            ex = list(lwA) if s2 == 0 else list(lwB)
            E("scalar", lambda e, src=src, s2=s2: e.activation(
                dump2[s2], src, AF.Square,
                accum_out=ssc[:, base + s2 * 8:base + s2 * 8 + 1]),
                waits=list(wsc) + ex,
                inc=("sc", 1), key=f"{tag}sqa")
        for s2, tag in enumerate((tagA, tagB)):
            E("scalar", lambda e, s2=s2: e.activation(ssc[:, base + s2 * 8 + 2:base + s2 * 8 + 3], ssc[:, base + s2 * 8:base + s2 * 8 + 1],
                                                      AF.Identity), waits=[W(f"{tag}sqa")], inc=("sc", 1), key=f"{tag}sq")
        for s2, tag in enumerate((tagA, tagB)):
            E("vector", lambda e, s2=s2: e.tensor_scalar(svec[:, base + s2 * 8 + 2:base + s2 * 8 + 3], svec[:, base + s2 * 8:base + s2 * 8 + 1],
                                                         1.0 / 1024, None, ALU.mult),
              waits=[W(f"{tag}sum0")], inc=("vec", 1), key=f"{tag}m")
        for s2, tag in enumerate((tagA, tagB)):
            E("vector", lambda e, s2=s2: e.tensor_mul(svec[:, base + s2 * 8 + 3:base + s2 * 8 + 4], svec[:, base + s2 * 8 + 2:base + s2 * 8 + 3],
                                                      svec[:, base + s2 * 8 + 2:base + s2 * 8 + 3]),
              waits=[W(f"{tag}m")], inc=("vec", 1), key=f"{tag}m2")
        for s2, tag in enumerate((tagA, tagB)):
            E("vector", lambda e, s2=s2: e.scalar_tensor_tensor(svec[:, base + s2 * 8 + 4:base + s2 * 8 + 5],
                                                                ssc[:, base + s2 * 8 + 2:base + s2 * 8 + 3], 1.0 / 1024,
                                                                svec[:, base + s2 * 8 + 3:base + s2 * 8 + 4], ALU.mult, ALU.subtract),
              waits=[W(f"{tag}m2"), W(f"{tag}sq")], inc=("vec", 1), key=f"{tag}var")
        for s2, tag in enumerate((tagA, tagB)):
            E("scalar", lambda e, s2=s2: e.activation(ssc[:, base + s2 * 8 + 4:base + s2 * 8 + 5], svec[:, base + s2 * 8 + 4:base + s2 * 8 + 5],
                                                      AF.Ln, bias=eps_sb[:, 0:1]),
              waits=[W(f"{tag}var")], inc=("sc", 1), key=f"{tag}ln")
        for s2, tag in enumerate((tagA, tagB)):
            E("scalar", lambda e, s2=s2: e.activation(ssc[:, base + s2 * 8 + 5:base + s2 * 8 + 6], ssc[:, base + s2 * 8 + 4:base + s2 * 8 + 5],
                                                      AF.Exp, scale=-0.5),
              waits=[W(f"{tag}ln")], inc=("sc", 1), key=f"{tag}r")
        for s2, tag in enumerate((tagA, tagB)):
            E("vector", lambda e, s2=s2: e.scalar_tensor_tensor(svec[:, base + s2 * 8 + 5:base + s2 * 8 + 6],
                                                                svec[:, base + s2 * 8 + 2:base + s2 * 8 + 3], -1.0,
                                                                ssc[:, base + s2 * 8 + 5:base + s2 * 8 + 6], ALU.mult, ALU.mult),
              waits=[W(f"{tag}r")], inc=("vec", 1), key=f"{tag}nmr")
            out[tag] = (ssc[:, base + s2 * 8 + 5:base + s2 * 8 + 6], svec[:, base + s2 * 8 + 5:base + s2 * 8 + 6], f"{tag}nmr")
        return out

    def ln_pipe8(mktag, srcs8, wv8, ws8, dump_ap, pcol=0, coff=0, n8=8):
        res = {}
        R = range(n8)
        sC = lambda base, i: svec[:, base + coff + i: base + coff + i + 1]
        cC = lambda base, i: ssc[:, base + coff + i: base + coff + i + 1]
        for tt in R:
            E("vector", lambda e, tt=tt: e.reduce_sum(sC(8, tt), srcs8[tt], axis=AX.X),
              waits=wv8[tt], inc=("vec", 1), key=f"{mktag}{tt}sum")
        for tt in R:
            waw = [W(f"{mktag}{tt-1}sqa")] if tt > 0 else []
            E("scalar", lambda e, tt=tt: e.activation(dump_ap, srcs8[tt], AF.Square,
                                                      accum_out=cC(8, tt)),
              waits=list(ws8[tt]) + waw, inc=("sc", 1), key=f"{mktag}{tt}sqa")
        for tt in R:
            E("scalar", lambda e, tt=tt: e.activation(cC(16, tt), cC(8, tt), AF.Identity),
              waits=[W(f"{mktag}{tt}sqa")], inc=("sc", 1), key=f"{mktag}{tt}sq")
        for tt in R:
            E("vector", lambda e, tt=tt: e.tensor_scalar(sC(16, tt), sC(8, tt),
                                                         1.0 / 1024, None, ALU.mult),
              waits=[W(f"{mktag}{tt}sum")], inc=("vec", 1), key=f"{mktag}{tt}m")
        for tt in R:
            E("vector", lambda e, tt=tt: e.tensor_mul(sC(24, tt), sC(16, tt), sC(16, tt)),
              waits=[W(f"{mktag}{tt}m")], inc=("vec", 1), key=f"{mktag}{tt}m2")
        for tt in R:
            E("vector", lambda e, tt=tt: e.scalar_tensor_tensor(sC(24, tt), cC(16, tt),
                                                                1.0 / 1024, sC(24, tt),
                                                                ALU.mult, ALU.subtract),
              waits=[W(f"{mktag}{tt}m2"), W(f"{mktag}{tt}sq")], inc=("vec", 1), key=f"{mktag}{tt}var")
        for tt in R:
            E("scalar", lambda e, tt=tt: e.activation(cC(24, tt), sC(24, tt), AF.Ln,
                                                      bias=eps_sb[:, 0:1]),
              waits=[W(f"{mktag}{tt}var")], inc=("sc", 1), key=f"{mktag}{tt}ln")
        for tt in R:
            E("scalar", lambda e, tt=tt: e.activation(ssc[:, pcol + tt:pcol + tt + 1], cC(24, tt),
                                                      AF.Exp, scale=-0.5),
              waits=[W(f"{mktag}{tt}ln")], inc=("sc", 1), key=f"{mktag}{tt}r")
        for tt in R:
            E("vector", lambda e, tt=tt: e.scalar_tensor_tensor(svec[:, pcol + tt:pcol + tt + 1],
                                                                sC(16, tt), -1.0,
                                                                ssc[:, pcol + tt:pcol + tt + 1],
                                                                ALU.mult, ALU.mult),
              waits=[W(f"{mktag}{tt}r")], inc=("vec", 1), key=f"{mktag}{tt}nmr")
            res[tt] = (ssc[:, pcol + tt:pcol + tt + 1], svec[:, pcol + tt:pcol + tt + 1], f"{mktag}{tt}nmr")
        return res

    gfb_count = 0
    for q in range(NQ):
        T = f"q{q}"
        if q == 0:
            aw_waits = []
        elif PHASEB:
            aw_waits = [W(f"s2end_q{q-1}")]
        else:
            aw_waits = [W(f"q{q-1}c{NCH-1}wo11")]
        for nm, base, src, semn in (("wq", WQ0, d["wq"], "awq"), ("wk", WK0, d["wk"], "awk"),
                                    ("wv", WV0, d["wv"], "awv"), ("wo", WO0, d["wo"], "awo")):
            dst = bigA[:, base:base + 8192].rearrange("p (hc h2) -> p hc h2", hc=NHC)
            E("gpsimd", lambda e, dst=dst, src=src: e.dma_start(
                out=dst, in_=src.rearrange("(hc p) h2 -> p hc h2", p=128)),
                waits=aw_waits, inc=(semn, 16), key=f"{T}{nm}")
            aw_waits = []
        for ee in range(NE if PHASEB else 0):
            for fb in range(NFB):
                gfb = gfb_count
                gfb_count += 1
                ws = []
                if gfb >= NWB:
                    ws = [W(f"s1fb{gfb - NWB}")]
                bi = gfb % NWB
                for wsrc, wdstb in ((d["wgate"], wgb), (d["wup"], wub)):
                    src = wsrc[ee].rearrange("(hc p) f -> p hc f", p=128)[:, :, fb * FB:(fb + 1) * FB]
                    dstv = wdstb[bi][:].rearrange("p (hc f) -> p hc f", hc=NHC)
                    E("gpsimd", lambda e, dstv=dstv, src=src: e.dma_start(out=dstv, in_=src),
                      waits=ws, inc=(f"wgu{gfb % NWB}", 16), key=f"wgu{gfb}" if wsrc is d["wup"] else None)
                    ws = []

        if not PHASEA:
            for c in range(NCH):
                C = f"{T}c{c}"
                xw = [W(f"store_q{q-1}")] if (q > 0 and c == 0) else []
                E("sync", lambda e, q=q, c=c: e.dma_start(out=yv[:, 2 * c:2 * c + 2, :], in_=x5[q, c]),
                  waits=xw, inc=("xs0", 16), key=f"{C}x")
        for c in range(NCH if PHASEA else 0):
            C = f"{T}c{c}"
            xw = [W(f"store_q{q-1}_tt{2*c+1}")] if q > 0 else []
            gc = q * NCH + c
            if gc >= 2:
                pq, pc = divmod(gc - 2, NCH)
                xw += [W(f"q{pq}c{pc}x1t1"), W(f"q{pq}c{pc}l1t1sum0")]
            E("sync", lambda e, q=q, c=c: e.dma_start(out=yv[:, 2 * c:2 * c + 2, :], in_=x5[q, c]),
              waits=xw, inc=(f"xs{c % 2}", 16), key=f"{C}x")

            if c == 0:
                ln1pend = {}

                def emit_ln1(c2, q=q, T=T, ln1pend=None):
                    C2 = f"{T}c{c2}"
                    lwA, lwB = [], []
                    if c2 > 0:
                        if PHA_DEPTH >= 3:
                            lwA = [W(f"{T}c{c2-1}pt{NH-1}")]
                            lwB = [W(f"{T}c{c2-1}norm{NH-1}")]
                    elif q > 0:
                        if PHA_DEPTH >= 3:
                            lwA = [W(f"q{q-1}c{NCH-1}pt{NH-1}")]
                            lwB = [W(f"q{q-1}c{NCH-1}norm{NH-1}")]
                        if PHASEB:
                            kbl = q * NE * NFC * 2 // 2
                            lwA = lwA + [W(f"wddma{kbl-1}"), W(f"wddma{kbl-2}")]
                            lwB = lwB + [W(f"wddma{kbl-1}"), W(f"wddma{kbl-2}")]
                    ln1pend[c2] = lnchain2(
                        f"{C2}l1t0", f"{C2}l1t1",
                        y[:, (2 * c2) * H:(2 * c2 + 1) * H], y[:, (2 * c2 + 1) * H:(2 * c2 + 2) * H],
                        [W(f"{C2}x")], [W(f"{C2}x")], [], [],
                        (dumpA, s_sb), lwA=lwA, lwB=lwB)
                if q == 0 or not (PHASEA and PHASEB and PHA_DEPTH >= 4):
                    emit_ln1(0, ln1pend=ln1pend)
            early = (c == 0 and q > 0 and PHASEA and PHASEB and PHA_DEPTH >= 4)
            rres1 = None if early else ln1pend[c]
            for tt2 in range(2):
                tt = 2 * c + tt2
                ytile = y[:, tt * H:(tt + 1) * H]
                x1b = x1buf if tt2 == 0 else dumpA
                if not early:
                    lw_pe = []
                    if tt2 == 0:
                        if c > 0:
                            lw_pe = [W(f"{T}c{c-1}trg01")]
                        elif q > 0:
                            lw_pe = [W(f"q{q-1}b{NTT-1}x2tr1") if PHASEB else W(f"q{q-1}c{NCH-1}trg01")]
                    else:
                        if PHA_DEPTH >= 3:
                            if c > 0:
                                lw_pe = [W(f"{T}c{c-1}pt{NH-1}"), W(f"{T}c{c-1}norm{NH-1}")]
                            elif q > 0:
                                lw_pe = [W(f"q{q-1}c{NCH-1}pt{NH-1}"), W(f"q{q-1}c{NCH-1}norm{NH-1}")]
                        if c == 0 and q == 0:
                            lw_pe = []
                        if c > 0 or q > 0:
                            lw_pe = lw_pe + [W(f"{T}c{c-1}trg11") if c > 0 else
                                             (W(f"q{q-1}c{NCH-1}trg11") if PHASEA else W(f"q{q-1}c{NCH-1}x"))]
                    r1, nmr1, k_nmr1 = rres1[f"{C}l1t{tt2}"]
                    E("scalar", lambda e, ytile=ytile, r1=r1, nmr1=nmr1, x1b=x1b: e.activation(
                        x1b, ytile, AF.Identity, bias=nmr1, scale=r1),
                        waits=[W(k_nmr1)] + lw_pe, inc=("sc", 1), key=f"{C}x1t{tt2}")
                for g2 in range(2):
                    bank = g2
                    for k4 in range(4):
                        hc = g2 * 4 + k4
                        ws = []
                        if k4 == 0:
                            ws = [W(f"{C}x1t{tt2}")] + b.bankw(bank) + ([W("identf")] if (q == 0 and c == 0 and tt2 == 0) else [])
                        E("tensor", lambda e, bank=bank, k4=k4, hc=hc, x1b=x1b: e.transpose(
                            P[bank][:, k4 * 128:(k4 + 1) * 128], x1b[:, hc * 128:(hc + 1) * 128], identf_sb[:]),
                            waits=ws, inc=("pe", 1) if k4 == 3 else None, key=f"{C}trg{tt2}{g2}" if k4 == 3 else None)
                    if g2 == 1:
                        b.ev[f"{C}tr{tt2}"] = b.ev[f"{C}trg{tt2}{g2}"]
                    E("vector", lambda e, g2=g2, tt2=tt2, bank=bank: e.tensor_copy(
                        x1Tv[:, g2 * 4:(g2 + 1) * 4, tt2, :],
                        P[bank][:, 0:512].rearrange("p (k f) -> p k f", k=4)),
                        waits=[W(f"{C}trg{tt2}{g2}")], inc=("vec", 1), key=f"{C}x1T{tt2}{g2}")
                    b.setbank(bank, W(f"{C}x1T{tt2}{g2}"))

            if PHA_DEPTH < 2:
                if c + 1 < NCH:
                    emit_ln1(c + 1, ln1pend=ln1pend)
                continue
            gi = 0
            for proj in ("q", "k"):
                wbase = WQ0 if proj == "q" else WK0
                dstT = qT if proj == "q" else kT
                for dq in range(8):
                    bank = 4 + gi % 2
                    for hc in range(NHC):
                        ws = []
                        if hc == 0:
                            ws = [W(f"{C}x1T11")] + b.bankw(bank)
                            if c == 0 and gi == 0:
                                ws.append(W(f"{T}wq"))
                            if c == 0 and proj == "k" and dq == 0:
                                ws.append(W(f"{T}wk"))
                        E("tensor", lambda e, bank=bank, hc=hc, dq=dq, wbase=wbase: e.matmul(
                            P[bank][:, 0:256], bigA[:, wbase + hc * 1024 + dq * 128: wbase + hc * 1024 + (dq + 1) * 128],
                            x1T[:, hc * 256:(hc + 1) * 256], start=(hc == 0), stop=(hc == NHC - 1)),
                            waits=ws, inc=("pe", 1) if hc == NHC - 1 else None,
                            key=f"{C}prj{gi}" if hc == NHC - 1 else None)
                    if proj == "q":
                        E("scalar", lambda e, bank=bank, dq=dq: e.activation(
                            qT[:, dq * 256:(dq + 1) * 256], P[bank][:, 0:256], AF.Identity, scale=QSCL),
                            waits=[W(f"{C}prj{gi}")], inc=("sc", 1), key=f"{C}qev{dq}")
                        b.setbank(bank, W(f"{C}qev{dq}"))
                    else:
                        E("vector", lambda e, bank=bank, dq=dq: e.tensor_copy(
                            kT[:, dq * 256:(dq + 1) * 256], P[bank][:, 0:256]),
                            waits=[W(f"{C}prj{gi}")], inc=("vec", 1), key=f"{C}kev{dq}")
                        b.setbank(bank, W(f"{C}kev{dq}"))
                    gi += 1
            def emit_scores(h):
                par = h % 2
                dq = h // 2
                sbank = par
                bw = [W(f"{C}mask{h - 2}")] if h >= 2 else b.bankw(sbank)
                for it in range(2):
                    ws = []
                    if it == 0:
                        ws = [W(f"{C}qev{dq}"), W(f"{C}kev{dq}")] + bw
                    E("tensor", lambda e, sbank=sbank, it=it, par=par, dq=dq: e.matmul(
                        P[sbank][:, it * 256:(it + 1) * 256],
                        qT[64 * par:64 * par + 64, dq * 256 + it * 128: dq * 256 + it * 128 + 128],
                        kT[64 * par:64 * par + 64, dq * 256:(dq + 1) * 256],
                        start=True, stop=True),
                        waits=ws, inc=("pe", 1) if it == 1 else None, key=f"{C}sc{h}" if it == 1 else None)

            for tt2 in range(2):
                for dvh in range(2):
                    bank = 4 + gi % 2
                    for hc in range(NHC):
                        ws = []
                        if hc == 0:
                            ws = b.bankw(bank)
                            if c == 0 and tt2 == 0 and dvh == 0:
                                ws.append(W(f"{T}wv"))
                        E("tensor", lambda e, bank=bank, hc=hc, tt2=tt2, dvh=dvh: e.matmul(
                            P[bank][:, 0:512], x1T[:, hc * 256 + tt2 * 128: hc * 256 + tt2 * 128 + 128],
                            bigA[:, WV0 + hc * 1024 + dvh * 512: WV0 + hc * 1024 + (dvh + 1) * 512],
                            start=(hc == 0), stop=(hc == NHC - 1)),
                            waits=ws, inc=("pe", 1) if hc == NHC - 1 else None,
                            key=f"{C}prj{gi}" if hc == NHC - 1 else None)
                    E("vector", lambda e, bank=bank, tt2=tt2, dvh=dvh: e.tensor_copy(
                        v_sb[:, tt2 * 1024 + dvh * 512: tt2 * 1024 + (dvh + 1) * 512], P[bank][:, 0:512]),
                        waits=[W(f"{C}prj{gi}")], inc=("vec", 1), key=f"{C}vev{tt2}{dvh}")
                    b.setbank(bank, W(f"{C}vev{tt2}{dvh}"))
                    gi += 1

            if PHA_DEPTH < 3:
                if c + 1 < NCH:
                    emit_ln1(c + 1, ln1pend=ln1pend)
                continue
            emit_scores(0)
            emit_scores(1)

            def emit_front(h):
                par = h % 2
                sbank = par
                s_h = s_sb[:, par * 512:(par + 1) * 512]
                pn_h = pn[:, par * 512:(par + 1) * 512]
                E("vector", lambda e, sbank=sbank, s_h=s_h: e.tensor_add(s_h, P[sbank][:, 0:512], mask_sb[:]),
                  waits=[W(f"{C}sc{h}")] + ([W("mask")] if (q == 0 and c == 0 and h == 0) else []),
                  inc=("vec", 1), key=f"{C}mask{h}")
                b.setbank(sbank, W(f"{C}mask{h}"))
                for it in range(2):
                    E("vector", lambda e, s_h=s_h, it=it, par=par: e.reduce_max(
                        svec[:, 16 + 4 * par + it: 17 + 4 * par + it], s_h[:, it * 256:(it + 1) * 256],
                        axis=AX.X, negate=True),
                        waits=[W(f"{C}mask{h}")] if it == 0 else (), inc=("vec", 1), key=f"{C}max{h}{it}")
                for it in range(2):
                    ws = [W(f"{C}max{h}{it}")]
                    if it == 0 and h >= 2:
                        ws.append(W(f"{C}pt{h - 2}"))
                    E("scalar", lambda e, pn_h=pn_h, s_h=s_h, it=it, par=par: e.activation(
                        pn_h[:, it * 256:(it + 1) * 256], s_h[:, it * 256:(it + 1) * 256], AF.Exp,
                        bias=svec[:, 16 + 4 * par + it: 17 + 4 * par + it],
                        accum_out=ssc[:, 16 + 4 * par + it: 17 + 4 * par + it]),
                        waits=ws,
                        inc=("sc", 1) if it == 1 else None, key=f"{C}expa{h}" if it == 1 else None)
                E("scalar", lambda e, par=par: e.activation(
                    ssc[:, 18 + 4 * par: 20 + 4 * par], ssc[:, 16 + 4 * par: 18 + 4 * par], AF.Identity),
                    waits=[W(f"{C}expa{h}")], inc=("sc", 1), key=f"{C}sum{h}")

            def emit_mid(h):
                par = h % 2
                ptb = par
                pn_h = pn[:, par * 512:(par + 1) * 512]
                for it in range(2):
                    E("vector", lambda e, it=it, par=par: e.reciprocal(
                        svec[:, 18 + 4 * par + it: 19 + 4 * par + it], ssc[:, 18 + 4 * par + it: 19 + 4 * par + it]),
                        waits=[W(f"{C}sum{h}")] if it == 0 else (), inc=("vec", 1), key=f"{C}rcp{h}{it}")
                for it in range(2):
                    E("vector", lambda e, pn_h=pn_h, it=it, par=par: e.tensor_scalar(
                        pn_h[:, it * 256:(it + 1) * 256], pn_h[:, it * 256:(it + 1) * 256],
                        svec[:, 18 + 4 * par + it: 19 + 4 * par + it], None, ALU.mult),
                        waits=[W(f"{C}rcp{h}{it}")], inc=("vec", 1) if it == 1 else None,
                        key=f"{C}norm{h}" if it == 1 else None)
                for it in range(2):
                    for jt in range(2):
                        ws = []
                        if it == 0 and jt == 0:
                            ws = [W(f"{C}norm{h}")]
                            ws += [W(f"{C}ptev{h - 2}")] if h >= 2 else b.bankw(6 + ptb)
                            ws += [W("identb")] if (q == 0 and c == 0 and h < 2) else []
                        E("tensor", lambda e, ptb=ptb, it=it, jt=jt, pn_h=pn_h: e.transpose(
                            Pb[ptb][:, (2 * it + jt) * 128:(2 * it + jt + 1) * 128],
                            pn_h[:, it * 256 + jt * 128: it * 256 + jt * 128 + 128], identb_sb[:]),
                            waits=ws, inc=("pe", 1) if (it == 1 and jt == 1) else None,
                            key=f"{C}pt{h}" if (it == 1 and jt == 1) else None)

            def emit_back(h):
                par = h % 2
                dq = h // 2
                pvbank = 2 + par
                ptb = par
                E("vector", lambda e, ptb=ptb: e.tensor_copy(
                    pT[ptb].rearrange("p (jt it f) -> p jt it f", jt=2, it=2),
                    Pb[ptb][:, 0:512].rearrange("p (it jt f) -> p jt it f", it=2, jt=2)),
                    waits=[W(f"{C}pt{h}")], inc=("vec", 1), key=f"{C}ptev{h}")
                b.setbank(6 + ptb, W(f"{C}ptev{h}"))
                for jt in range(2):
                    ws = []
                    if jt == 0:
                        ws = [W(f"{C}ptev{h}"), W(f"{C}vev1{h // 8}")]
                        ws += [W(f"{C}atev{h - 2}")] if h >= 2 else b.bankw(pvbank)
                    E("tensor", lambda e, pvbank=pvbank, jt=jt, h=h, ptb=ptb: e.matmul(
                        P[pvbank][0:64, 0:256], v_sb[:, jt * 1024 + h * 64: jt * 1024 + h * 64 + 64],
                        pT[ptb][:, jt * 256:(jt + 1) * 256], start=(jt == 0), stop=(jt == 1)),
                        waits=ws, inc=("pe", 1) if jt == 1 else None, key=f"{C}pv{h}" if jt == 1 else None)
                E("scalar", lambda e, pvbank=pvbank, par=par, dq=dq: e.activation(
                    attnT[64 * par:64 * par + 64, dq * 256:(dq + 1) * 256], P[pvbank][0:64, 0:256], AF.Identity),
                    waits=[W(f"{C}pv{h}")], inc=("sc", 1), key=f"{C}atev{h}")
                b.setbank(pvbank, W(f"{C}atev{h}"))

            for hh in range(NH + 2):
                if hh + 2 < NH:
                    emit_scores(hh + 2)
                if hh < NH:
                    emit_front(hh)
                if 0 <= hh - 1 < NH:
                    emit_mid(hh - 1)
                if 0 <= hh - 2 < NH:
                    emit_back(hh - 2)

            if c + 1 < NCH:
                emit_ln1(c + 1, ln1pend=ln1pend)

            if PHA_DEPTH < 4:
                continue
            for tt2 in range(2):
                tt = 2 * c + tt2
                for hoh in range(2):
                    bank = 4 + hoh
                    for dc in range(NHC):
                        ws = [W(f"{C}atev{min(2 * dc + 1, NH - 1)}")]
                        if dc == 0:
                            ws += b.bankw(bank)
                            if c == 0 and tt2 == 0 and hoh == 0:
                                ws.append(W(f"{T}wo"))
                        E("tensor", lambda e, bank=bank, dc=dc, tt2=tt2, hoh=hoh: e.matmul(
                            P[bank][:, 0:512], attnT[:, dc * 256 + tt2 * 128: dc * 256 + tt2 * 128 + 128],
                            bigA[:, WO0 + dc * 1024 + hoh * 512: WO0 + dc * 1024 + (hoh + 1) * 512],
                            start=(dc == 0), stop=(dc == NHC - 1)),
                            waits=ws, inc=("pe", 1) if dc == NHC - 1 else None,
                            key=f"{C}wo{tt2}{hoh}" if dc == NHC - 1 else None)
                r2, nmr2, k_nmr2, k_r2, m2 = lnchain(
                    f"{C}l2t{tt2}", q, tt2, [(P[4][:, 0:512], 512), (P[5][:, 0:512], 512)],
                    src_waits_v=[W(f"{C}wo{tt2}0"), W(f"{C}wo{tt2}1")],
                    src_waits_s=[W(f"{C}wo{tt2}0"), W(f"{C}wo{tt2}1")], dump=dumpA, psum_src=True)
                ysl = y[:, tt * H:(tt + 1) * H]
                yaw = [W(k_nmr2)]
                if c == 0 and q > 0:
                    yaw.append(W(f"{C}x"))
                E("vector", lambda e, ysl=ysl, nmr2=nmr2: e.tensor_scalar(ysl, ysl, nmr2, None, ALU.add),
                  waits=yaw, inc=("vec", 1), key=f"{C}ya{tt2}")
                for hoh in range(2):
                    E("vector", lambda e, tt=tt, hoh=hoh, r2=r2: e.scalar_tensor_tensor(
                        y[:, tt * H + hoh * 512: tt * H + (hoh + 1) * 512], P[4 + hoh][:, 0:512], r2,
                        y[:, tt * H + hoh * 512: tt * H + (hoh + 1) * 512], ALU.mult, ALU.add),
                        waits=[W(f"{C}ya{tt2}")] if hoh == 0 else [W(f"{C}yr{tt2}0")],
                        inc=("vec", 1), key=f"{C}yr{tt2}{hoh}")
                    b.setbank(4 + hoh, W(f"{C}yr{tt2}{hoh}"))

        markA_pe = b.c.get("pe", 0)
        markA_vec = b.c.get("vec", 0)
        markA_sc = b.c.get("sc", 0)

        if not PHASEB:
            lastk = {1: f"{T}c{NCH-1}x1T11", 2: f"{T}c{NCH-1}vev11",
                     3: f"{T}c{NCH-1}atev{NH-1}", 4: f"{T}c{NCH-1}yr11"}[PHA_DEPTH]
            E("scalar", lambda e, q=q: e.dma_start(out=out4[q], in_=yv),
              waits=[W(lastk)], inc=("st", 16), key=f"store_q{q}")
            b.ev[f"s2end_q{q}"] = b.ev[lastk if PHA_DEPTH < 4 else f"{T}c{NCH-1}wo11"]
            continue
        srcs8 = [y[:, tt * H:(tt + 1) * H] for tt in range(NTT)]
        if PHASEA:
            wvA = [[W(f"{T}c0yr01")], [W(f"{T}c0yr11")]]
            wsA = [[W(f"{T}c0yr01"), ("abs", "pe", markA_pe), ("abs", "vec", markA_vec)], []]
            rA = ln_pipe8(f"{T}l3a_", srcs8[0:2], wvA, wsA, dump3, pcol=0, coff=0, n8=2)
            wvB = [[W(f"{T}c{(tt+2)//2}yr{tt%2}1")] for tt in range(6)]
            wsB = [[] for tt in range(6)]
            rB = ln_pipe8(f"{T}l3b_", srcs8[2:8], wvB, wsB, dump3, pcol=2, coff=2, n8=6)
            rres3 = {0: rA[0], 1: rA[1]}
            for i in range(6):
                rres3[i + 2] = rB[i]
        else:
            ykey = f"{T}c{NCH-1}x"
            wv8 = [[W(ykey)] if tt == 0 else [] for tt in range(NTT)]
            ws8 = [[W(ykey), ("abs", "pe", markA_pe), ("abs", "vec", markA_vec)] if tt == 0 else []
                   for tt in range(NTT)]
            rres3 = ln_pipe8(f"{T}l3_", srcs8, wv8, ws8, dump3)
        for tt in range(NTT):
            B2 = f"{T}b{tt}"
            ysl = y[:, tt * H:(tt + 1) * H]
            x2b = x2buf if tt % 2 == 0 else x2buf2
            r3, nmr3, k_nmr3 = rres3[tt]
            E("scalar", lambda e, ysl=ysl, r3=r3, nmr3=nmr3, x2b=x2b: e.activation(
                x2b, ysl, AF.Identity, bias=nmr3, scale=r3),
                waits=[W(k_nmr3)] + ([W(f"{T}b{tt-2}x2tr1")] if tt >= 2 else
                                      [("abs", "pe", markA_pe), ("abs", "vec", markA_vec)]),
                inc=("sc", 1), key=f"{B2}x2")
            for g2 in range(2):
                bank = g2
                for k4 in range(4):
                    hc = g2 * 4 + k4
                    ws = []
                    if k4 == 0:
                        ws = [W(f"{B2}x2")] + b.bankw(bank)
                    E("tensor", lambda e, bank=bank, k4=k4, hc=hc, x2b=x2b: e.transpose(
                        P[bank][:, k4 * 128:(k4 + 1) * 128], x2b[:, hc * 128:(hc + 1) * 128], identf_sb[:]),
                        waits=ws, inc=("pe", 1) if k4 == 3 else None, key=f"{B2}x2tr{g2}" if k4 == 3 else None)
                E("vector", lambda e, g2=g2, tt=tt, bank=bank: e.tensor_copy(
                    x2Tv[:, g2 * 4:(g2 + 1) * 4, tt * 128:(tt + 1) * 128],
                    P[bank][:, 0:512].rearrange("p (k f) -> p k f", k=4)),
                    waits=[W(f"{B2}x2tr{g2}")], inc=("vec", 1), key=f"{B2}x2T{g2}")
                b.setbank(bank, W(f"{B2}x2T{g2}"))
            for hc in range(NHC):
                ws = []
                if hc == 0:
                    ws = [W(f"{B2}x2T1")] + b.bankw(2) + ([W("wrg")] if (q == 0 and tt == 0) else [])
                E("tensor", lambda e, hc=hc, tt=tt: e.matmul(
                    P[2][:, 0:16], x2T[:, hc * TQ + tt * 128: hc * TQ + tt * 128 + 128],
                    wrgb[:, hc * 16:(hc + 1) * 16], start=(hc == 0), stop=(hc == NHC - 1)),
                    waits=ws, inc=("pe", 1) if hc == NHC - 1 else None, key=f"{B2}rg" if hc == NHC - 1 else None)
            E("vector", lambda e: e.reduce_max(svec[:, 24:25], P[2][:, 0:8], axis=AX.X, negate=True),
              waits=[W(f"{B2}rg")], inc=("vec", 1), key=f"{B2}mxA")
            E("vector", lambda e: e.reduce_max(svec[:, 25:26], P[2][:, 8:16], axis=AX.X, negate=True),
              inc=("vec", 1), key=f"{B2}mxB")
            E("vector", lambda e: e.tensor_scalar(svec[:, 26:27], svec[:, 25:26], 10.0, None, ALU.mult),
              waits=[W(f"{B2}mxB")], inc=("vec", 1), key=f"{B2}mxB10")
            E("scalar", lambda e: e.activation(rg_sb[:, 0:8], P[2][:, 0:8], AF.Exp,
                                               bias=svec[:, 24:25], accum_out=ssc[:, 24:25]),
              waits=[W(f"{B2}mxA")])
            E("scalar", lambda e: e.activation(rg_sb[:, 8:16], P[2][:, 8:16], AF.Exp, scale=10.0,
                                               bias=svec[:, 26:27], accum_out=ssc[:, 25:26]),
              waits=[W(f"{B2}mxB10")], inc=("sc", 1), key=f"{B2}gexp")
            b.setbank(2, W(f"{B2}gexp"))
            E("scalar", lambda e: e.activation(ssc[:, 26:28], ssc[:, 24:26], AF.Identity),
              waits=[W(f"{B2}gexp")], inc=("sc", 1), key=f"{B2}sums")
            E("vector", lambda e: e.reciprocal(svec[:, 27:28], ssc[:, 26:27]),
              waits=[W(f"{B2}sums")], inc=("vec", 1), key=f"{B2}rca")
            E("vector", lambda e: e.reciprocal(svec[:, 28:29], ssc[:, 27:28]),
              inc=("vec", 1), key=f"{B2}rcb")
            E("vector", lambda e: e.tensor_scalar(sv8[:, 0:8], rg_sb[:, 0:8], svec[:, 27:28], None, ALU.mult),
              waits=[W(f"{B2}rca")], inc=("vec", 1), key=f"{B2}rp")
            E("vector", lambda e: e.tensor_scalar(sv8[:, 8:16], rg_sb[:, 8:16], svec[:, 28:29], None, ALU.mult),
              waits=[W(f"{B2}rcb")], inc=("vec", 1), key=f"{B2}gw")
            E("vector", lambda e: e.tensor_mul(sv8[:, 0:8], sv8[:, 0:8], sv8[:, 8:16]),
              waits=[W(f"{B2}rp"), W(f"{B2}gw")], inc=("vec", 1), key=f"{B2}p")
            E("vector", lambda e: e.reduce_sum(svec[:, 29:30], sv8[:, 0:8], axis=AX.X),
              waits=[W(f"{B2}p")], inc=("vec", 1), key=f"{B2}s3")
            E("vector", lambda e: e.tensor_scalar(svec[:, 30:31], svec[:, 29:30], 1e-9, None, ALU.add),
              waits=[W(f"{B2}s3")], inc=("vec", 1), key=f"{B2}s3b")
            E("vector", lambda e: e.reciprocal(svec[:, 31:32], svec[:, 30:31]),
              waits=[W(f"{B2}s3b")], inc=("vec", 1), key=f"{B2}s3r")
            E("vector", lambda e, tt=tt: e.tensor_scalar(cw_sb[:, tt * 8:(tt + 1) * 8], sv8[:, 0:8],
                                                         svec[:, 31:32], None, ALU.mult),
              waits=[W(f"{B2}s3r")], inc=("vec", 1), key=f"{B2}cw")

        wd_batch = 0
        for ee in range(NE):
            EB = f"{T}e{ee}"
            for fc in range(NFC):
                gidx = ee * NFC + fc
                s = gidx % 2
                fb = fc // 2
                gfb = (q * NE + ee) * NFB + fb
                fo = (fc % 2) * 128
                bufs = [wgb[gfb % NWB], wub[gfb % NWB]]
                for ui, (wbuf, gb0) in enumerate(((bufs[0], 4 * s), (bufs[1], 4 * s + 2))):
                    for hc in range(NHC):
                        for tb in range(2):
                            ws = []
                            if hc == 0 and tb == 0:
                                ws = [W(f"wgu{gfb}")] + b.bankw(gb0) + b.bankw(gb0 + 1)
                                if gidx == 0 and ui == 0:
                                    ws.append(W(f"{T}b{NTT-1}x2T1"))
                            E("tensor", lambda e, wbuf=wbuf, gb0=gb0, hc=hc, tb=tb, fo=fo: e.matmul(
                                P[gb0 + tb][:, 0:512],
                                wbuf[:, hc * FB + fo: hc * FB + fo + 128],
                                x2T[:, hc * TQ + tb * 512: hc * TQ + (tb + 1) * 512],
                                start=(hc == 0), stop=(hc == NHC - 1)),
                                waits=ws, inc=("pe", 1) if (hc == NHC - 1 and tb == 1) else None,
                                key=(f"{EB}s1g{fc}" if ui == 0 else f"{EB}s1u{fc}")
                                if (hc == NHC - 1 and tb == 1) else None)
                if fc % 2 == 1:
                    b.ev[f"s1fb{gfb}"] = b.ev[f"{EB}s1u{fc}"]
                for tb in range(2):
                    ws = [W(f"{EB}s1g{fc}")]
                    if gidx >= 2:
                        ws.append(W(f"{T}e{(gidx-2)//NFC}hm{(gidx-2)%NFC}_{tb}"))
                    E("scalar", lambda e, s=s, tb=tb: e.activation(sg[2 * s + tb], P[4 * s + tb][:, 0:512], AF.Silu),
                      waits=ws, inc=("sc", 1), key=f"{EB}silu{fc}_{tb}")
                    b.setbank(4 * s + tb, W(f"{EB}silu{fc}_{tb}"))
                for tb in range(2):
                    E("vector", lambda e, fc=fc, tb=tb, s=s: e.tensor_mul(
                        bigA[:, fc * 1024 + tb * 512: fc * 1024 + (tb + 1) * 512],
                        sg[2 * s + tb], P[4 * s + 2 + tb][:, 0:512]),
                        waits=[W(f"{EB}silu{fc}_{tb}"), W(f"{EB}s1u{fc}")],
                        inc=("vec", 1), key=f"{EB}hm{fc}_{tb}")
                    b.setbank(4 * s + 2 + tb, W(f"{EB}hm{fc}_{tb}"))
            for hdh in range(2):
                for fc in range(NFC):
                    kb = (q * NE + ee) * 2 * (NFC // 2) + hdh * (NFC // 2) + fc // 2
                    if fc % 2 == 0:
                        ws = []
                        if kb >= 2:
                            ws.append(W(f"wdcast{kb - 2}"))
                        if ee == 0 and hdh == 0 and fc == 0:
                            ws += [("abs", "sc", markA_sc), ("abs", "vec", markA_vec)]
                        src = d["wdown"][ee][fc * 128:(fc + 2) * 128, hdh * 512:(hdh + 1) * 512]
                        src3 = src.rearrange("(f2 p) n -> p f2 n", p=128)
                        E("sync", lambda e, kb=kb, src3=src3: e.dma_start(
                            out=wdst[kb % 2][:].rearrange("p (f2 n) -> p f2 n", f2=2), in_=src3),
                            waits=ws, inc=(f"wd{kb % 2}", 16), key=f"wddma{kb}")
                        ws = [W(f"wddma{kb}")]
                        if kb >= 2:
                            ws.append(W(f"s2f_{kb - 2}"))
                        E("scalar", lambda e, kb=kb: e.copy(
                            wdbf[:, (kb % 2) * 1024:(kb % 2 + 1) * 1024], wdst[kb % 2][:]),
                            waits=ws, inc=("sc", 1), key=f"wdcast{kb}")
                    slot = (kb % 2) * 1024 + (fc % 2) * 512
                    for tt in range(NTT):
                        ws = []
                        if tt == 0:
                            ws = [W(f"wdcast{kb}"), W(f"{EB}hm{fc}_1")]
                        if fc == 0:
                            ws = ws + b.bankw(tt)
                        E("tensor", lambda e, tt=tt, fc=fc, slot=slot: e.matmul(
                            P[tt][:, 0:512], bigA[:, fc * 1024 + tt * 128: fc * 1024 + tt * 128 + 128],
                            wdbf[:, slot:slot + 512], start=(fc == 0), stop=(fc == NFC - 1)),
                            waits=ws, inc=("pe", 1) if tt == NTT - 1 else None,
                            key=f"{EB}s2_{hdh}_{fc}" if tt == NTT - 1 else None)
                    if fc % 2 == 1:
                        b.ev[f"s2f_{kb}"] = b.ev[f"{EB}s2_{hdh}_{fc}"]
                for tt in range(NTT):
                    col = cw_sb[:, tt * 8 + ee: tt * 8 + ee + 1]
                    dst = accum[:, tt * H + hdh * 512: tt * H + (hdh + 1) * 512]
                    if ee == 0:
                        E("vector", lambda e, dst=dst, tt=tt, col=col: e.tensor_scalar(
                            dst, P[tt][:, 0:512], col, None, ALU.mult),
                            waits=[W(f"{EB}s2_{hdh}_{NFC-1}")], inc=("vec", 1), key=f"{EB}acc{hdh}_{tt}")
                    else:
                        E("vector", lambda e, dst=dst, tt=tt, col=col: e.scalar_tensor_tensor(
                            dst, P[tt][:, 0:512], col, dst, ALU.mult, ALU.add),
                            waits=[W(f"{EB}s2_{hdh}_{NFC-1}")], inc=("vec", 1), key=f"{EB}acc{hdh}_{tt}")
                    b.setbank(tt, W(f"{EB}acc{hdh}_{tt}"))
        b.ev[f"s2end_q{q}"] = b.ev[f"{T}e{NE-1}s2_1_{NFC-1}"]

        if q + 1 < NQ and PHASEA and PHASEB and PHA_DEPTH >= 4:
            nq = q + 1
            CN = f"q{nq}c0"
            xw = [W(f"q{q}c0x1t0"), W(f"q{q}c0x1t1")] if q > 0 else []
            E("gpsimd", lambda e, nq=nq: e.dma_start(
                out=xsc[:].rearrange("p (tt h) -> p tt h", tt=2), in_=x5[nq, 0]),
                waits=xw, inc=("xsc", 16), key=f"xscq{nq}")
            res_early = lnchain2(
                f"{CN}l1t0", f"{CN}l1t1",
                xsc[:, 0:1024], xsc[:, 1024:2048],
                [W(f"xscq{nq}")],
                [W(f"xscq{nq}"), W(f"{T}e{NE-1}hm{NFC-2}_1")],
                [], [W(f"{T}e{NE-1}hm{NFC-1}_1")],
                (arena[:, 3072:4096], arena[:, 4096:5120]))
            for tt2 in range(2):
                x1b = x1buf if tt2 == 0 else dumpA
                if tt2 == 0:
                    lw = [W(f"{T}b{NTT-1}x2tr1")]
                else:
                    lw = [W(f"{T}c{NCH-1}pt{NH-1}"), W(f"{T}c{NCH-1}norm{NH-1}"),
                          W(f"{T}c{NCH-1}trg11")]
                r1, nmr1, k_nmr1 = res_early[f"{CN}l1t{tt2}"]
                src = xsc[:, tt2 * 1024:(tt2 + 1) * 1024]
                E("scalar", lambda e, src=src, r1=r1, nmr1=nmr1, x1b=x1b: e.activation(
                    x1b, src, AF.Identity, bias=nmr1, scale=r1),
                    waits=[W(k_nmr1)] + lw, inc=("sc", 1), key=f"{CN}x1t{tt2}")

        srcs4 = [accum[:, tt * H:(tt + 1) * H] for tt in range(NTT)]
        wv4 = [[W(f"{T}e{NE-1}acc1_{tt}")] for tt in range(NTT)]
        ws4 = [[W(f"{T}e{NE-1}acc1_{tt}")] for tt in range(NTT)]
        rres4 = ln_pipe8(f"{T}l4_", srcs4, wv4, ws4, dump3, pcol=24)

        for tt in range(NTT):
            B4 = f"{T}o{tt}"
            atile = accum[:, tt * H:(tt + 1) * H]
            r4, nmr4, k_nmr4 = rres4[tt]
            ysl = y[:, tt * H:(tt + 1) * H]
            E("vector", lambda e, ysl=ysl, nmr4=nmr4: e.tensor_scalar(ysl, ysl, nmr4, None, ALU.add),
              waits=[W(k_nmr4)], inc=("vec", 1), key=f"{B4}ya")
            E("vector", lambda e, ysl=ysl, atile=atile, r4=r4: e.scalar_tensor_tensor(
                ysl, atile, r4, ysl, ALU.mult, ALU.add),
                waits=[W(f"{B4}ya")], inc=("vec", 1), key=f"{B4}out")
        for tt in range(NTT):
            stw = [W(f"{T}o{tt}out")]
            if q > 0 or tt > 0:
                stw.append(W(f"store_q{q}_tt{tt-1}" if tt > 0 else f"store_q{q-1}_tt{NTT-1}"))
            E("sync", lambda e, q=q, tt=tt: e.dma_start(out=out4[q][:, tt, :], in_=yv[:, tt, :]),
              waits=stw, inc=("st", 16), key=f"store_q{q}_tt{tt}")
        b.ev[f"store_q{q}"] = b.ev[f"store_q{q}_tt{NTT-1}"]

    E("sync", None, waits=[("abs", "st", NQ * NTT * 16)])

    def resolve(wspec):
        if wspec[0] == "ev":
            return b.ev[wspec[1]]
        if wspec[0] == "abs":
            return wspec[1], wspec[2]
        raise ValueError(wspec)

    with nc.Block() as block:
        def runner(ops):
            def f(eng):
                for waits, fn, inc in ops:
                    seen = {}
                    for wsp in waits:
                        sem, val = resolve(wsp)
                        if seen.get(sem, -1) >= val:
                            continue
                        seen[sem] = val
                        eng.wait_ge(sems[sem], val)
                    if fn is None:
                        continue
                    ins = fn(eng)
                    if inc:
                        ins.then_inc(sems[inc[0]], inc[1])
            return f
        block.sync(runner(b.ops["sync"]))
        block.gpsimd(runner(b.ops["gpsimd"]))
        block.tensor(runner(b.ops["tensor"]))
        block.vector(runner(b.ops["vector"]))
        block.scalar(runner(b.ops["scalar"]))

    ctx.close()
    return nc


def kernel(L0_attn_x, L0_attn_Wq, L0_attn_Wk, L0_attn_Wv, L0_attn_Wo,
           L0_ffn_w_router, L0_ffn_w_gating, L0_ffn_w_gate, L0_ffn_w_up, L0_ffn_w_down):
    x = np.ascontiguousarray(np.asarray(L0_attn_x, dtype=np.float32))
    consts = {
        "wq": np.ascontiguousarray(np.asarray(L0_attn_Wq, np.float32)),
        "wk": np.ascontiguousarray(np.asarray(L0_attn_Wk, np.float32)),
        "wv": np.ascontiguousarray(np.asarray(L0_attn_Wv, np.float32)),
        "wo": np.ascontiguousarray(np.asarray(L0_attn_Wo, np.float32)),
        "wr": np.ascontiguousarray(np.asarray(L0_ffn_w_router, np.float32)),
        "wgt": np.ascontiguousarray(np.asarray(L0_ffn_w_gating, np.float32)),
        "wgate": np.ascontiguousarray(np.asarray(L0_ffn_w_gate, np.float32)),
        "wup": np.ascontiguousarray(np.asarray(L0_ffn_w_up, np.float32)),
        "wdown": np.ascontiguousarray(np.asarray(L0_ffn_w_down, np.float32)),
        "identf": np.eye(128, dtype=np.float32),
        "identb": np.eye(128, dtype=np.float32).astype(ml_dtypes.bfloat16),
    }
    m = np.full((128, 512), MASKV, np.float32)
    jj = np.arange(256)
    for p in range(128):
        ok0 = (jj <= p) & (jj >= p - 64)
        ok1 = (jj <= 128 + p) & (jj >= 128 + p - 64)
        m[p, 0:256][ok0] = 0.0
        m[p, 256:512][ok1] = 0.0
    consts["mask"] = m

    nc = build_program()
    in_maps = []
    for core in range(NCORE):
        im = dict(consts)
        im["x"] = np.ascontiguousarray(x[core])
        in_maps.append(im)
    res = run_bass_kernel_spmd(nc, in_maps, core_ids=list(range(NCORE)))
    out = np.stack([res.results[core]["out"] for core in range(NCORE)], axis=0)
    return out
